# version 09
# baseline (speedup 1.0000x reference)
import math
import numpy as np
import concourse.bass as bass
import concourse.mybir as mybir
from concourse.bass_utils import run_bass_kernel_spmd

F32 = mybir.dt.float32
BF16 = mybir.dt.bfloat16
U8 = mybir.dt.uint8
ALU = mybir.AluOpType
AF = mybir.ActivationFunctionType

N_CORES = 8
B, S, D, H, DH, FF = 16, 2048, 1024, 16, 64, 4096
TOK = (B // N_CORES) * S
T = 512
NSLAB = TOK // T
NTT = T // 128
NC_D = D // 128
NC_F = FF // 128
NPAIR = H // 2
LN2 = math.log(2.0)
INV_LN2 = 1.0 / LN2

ENGS = ("sync", "vector", "scalar", "tensor", "gpsimd")
UNORDERED_SEMS = {"ld", "swq", "swk", "swv", "swo", "sw1", "sw2", "sw8", "w8l"}


class Res:
    __slots__ = ("w", "r")
    def __init__(self):
        self.w = None
        self.r = []


class Prog:
    def __init__(self):
        self.gidx = 0
        self.ops = {e: [] for e in ENGS}
        self.sem_h = {}
        self.sem_v = {}
        self.hw = {e: {} for e in ENGS}

    def sem(self, name, handle):
        self.sem_h[name] = handle
        self.sem_v[name] = 0

    def _app(self, eng, fn):
        self.ops[eng].append((self.gidx, fn))

    def _wait(self, eng, tok):
        if tok is None:
            return
        teng, sname, val, force = tok
        if sname in UNORDERED_SEMS:
            val = self.sem_v[sname]
        if teng == eng and not force:
            return
        if self.hw[eng].get(sname, 0) >= val:
            return
        self.hw[eng][sname] = val
        h = self.sem_h[sname]
        self._app(eng, lambda e, h=h, v=val: e.wait_ge(h, v))

    def group(self, eng, fns, reads=(), writes=(), sem=None, dma=False,
              force=False):
        self.gidx += 1
        for r in reads:
            self._wait(eng, r.w)
        for w in writes:
            self._wait(eng, w.w)
            for t in list(w.r):
                self._wait(eng, t)
        if sem is None:
            sem = {"sync": "sd", "vector": "vv", "scalar": "aa",
                   "tensor": "mm", "gpsimd": "gg"}[eng]
        if dma:
            force = True
        step = 16 if dma else 1
        self.sem_v[sem] += step
        val = self.sem_v[sem]
        h = self.sem_h[sem]
        for f in fns[:-1]:
            self._app(eng, f)
        last = fns[-1]
        self._app(eng, lambda e, f=last, h=h, s=step: f(e).then_inc(h, s))
        tok = (eng, sem, val, force)
        for r in reads:
            r.r.append(tok)
        for w in writes:
            w.w = tok
            w.r = []
        return tok


def build(nc):
    x_h = nc.dram_tensor("x", [TOK, D], F32, kind="ExternalInput").ap()
    y_h = nc.dram_tensor("y", [TOK, D], F32, kind="ExternalOutput").ap()
    wq_h = nc.dram_tensor("wq", [D, D], F32, kind="ExternalInput").ap()
    wk_h = nc.dram_tensor("wk", [D, D], F32, kind="ExternalInput").ap()
    wv_h = nc.dram_tensor("wv", [D, D], F32, kind="ExternalInput").ap()
    wo_h = nc.dram_tensor("wo", [D, D], F32, kind="ExternalInput").ap()
    proj_h = nc.dram_tensor("proj", [DH, DH], F32, kind="ExternalInput").ap()
    w1_h = nc.dram_tensor("w1", [D, FF], F32, kind="ExternalInput").ap()
    b1_h = nc.dram_tensor("b1", [1, FF], F32, kind="ExternalInput").ap()
    w2_h = nc.dram_tensor("w2", [FF, D], F32, kind="ExternalInput").ap()
    b2_h = nc.dram_tensor("b2", [1, D], F32, kind="ExternalInput").ap()
    g1_h = nc.dram_tensor("g1", [1, D], F32, kind="ExternalInput").ap()
    be1_h = nc.dram_tensor("be1", [1, D], F32, kind="ExternalInput").ap()
    g2_h = nc.dram_tensor("g2", [1, D], F32, kind="ExternalInput").ap()
    be2_h = nc.dram_tensor("be2", [1, D], F32, kind="ExternalInput").ap()
    w8_h = nc.dram_tensor("w8t", [D, D], U8, kind="ExternalInput").ap()
    fbias_h = nc.dram_tensor("fbias", [1, D], F32, kind="ExternalInput").ap()
    wsc_h = nc.dram_tensor("wsc", [128, 1], F32, kind="ExternalInput").ap()
    mask_h = nc.dram_tensor("mask2", [128, 128], F32, kind="ExternalInput").ap()
    wqs_h = nc.dram_tensor("wqs", [NC_D, NC_D, 128, 128], BF16).ap()
    wks_h = nc.dram_tensor("wks", [NC_D, NC_D, 128, 128], BF16).ap()
    wvs_h = nc.dram_tensor("wvs", [NC_D, 128, D], BF16).ap()
    wos_h = nc.dram_tensor("wos", [NC_D, 128, D], BF16).ap()
    w1s_h = nc.dram_tensor("w1s", [NC_F, NC_D, 128, 128], BF16).ap()
    w2s_h = nc.dram_tensor("w2s", [NC_F, 128, D], BF16).ap()
    w8s_h = nc.dram_tensor("w8s", [NC_D, 128, D], BF16).ap()
    DBG = getattr(build, "debug", False)
    if DBG:
        dbg = {n: nc.dram_tensor("dbg_" + n, [128, shp], dt, kind="ExternalOutput").ap()
               for n, shp, dt in (
                   ("htm0", D, BF16), ("qp", NC_D * T, BF16), ("kp", NC_D * T, BF16),
                   ("vpad", NTT * H * 65, BF16), ("atm0", D, BF16),
                   ("x2", NTT * D, F32), ("z", NTT * D, F32), ("x4", NTT * D, F32),
                   ("a0", D, BF16), ("hfm", NC_D * T, BF16), ("ktm", NC_D * T, BF16),
                   ("ufm", NC_F * T, BF16), ("kvb", NPAIR * 2 * 65, BF16),
                   ("qraw0", T, BF16), ("tq0", T, BF16), ("rq0", T, BF16),
                   ("st6", 12, F32), ("scc", 32, F32), ("htm1", D, BF16),
                   ("z5", NTT * D, F32), ("afm2", NC_D * T, BF16),
               )}

    import contextlib
    es = contextlib.ExitStack()
    sb = lambda name, shape, dt: es.enter_context(nc.sbuf_tensor(name, shape, dt))
    psb = lambda name, shape: es.enter_context(nc.psum_tensor(name, shape, F32))

    x_ab = sb("x_ab", [128, 2 * NTT * D], F32)
    x2b = sb("x2b", [128, NTT * D], F32)
    sx4 = sb("sx4", [128, NTT * D], F32)
    htm = sb("htm", [128, D], BF16)
    hfm = sb("hfm", [128, NC_D * T], BF16)
    qkraw = sb("qkraw", [128, 2 * T], BF16)
    qp = sb("qp", [128, NC_D * T], BF16)
    kp = sb("kp", [128, NC_D * T], BF16)
    tqr = sb("tqr", [128, 2 * T], BF16)
    vpad = sb("vpad", [128, NTT * H * 65], BF16)
    atm = sb("atm", [128, D], BF16)
    afm = sb("afm", [128, NC_D * T], BF16)
    ktmb = sb("ktmb", [128, NTT * D], BF16)
    kvs = sb("kvs", [128, NPAIR * 2 * 65], F32)
    kvb = sb("kvb", [128, NPAIR * 2 * 65], BF16)
    ufm = sb("ufm", [128, NC_F * T], BF16)
    qs = sb("qs", [128, 4 * D], F32)
    otmp = sb("otmp", [128, 4 * 65], F32)
    dsc = sb("dsc", [128, 32], F32)
    cb_w = sb("cb_w", [128, 3 * NC_D * 128], BF16)
    kc_w = sb("kc_w", [128, 4 * D], BF16)
    wq_w = wk_w = w1_w = cb_w
    wv_w = wo_w = w2_w = w8_w = kc_w
    fw = sb("fw", [128, 2 * 1024], F32)
    fu8 = sb("fu8", [128, 512], U8)
    g1c = sb("g1c", [128, NC_D], F32)
    be1c = sb("be1c", [128, NC_D], F32)
    g2c = sb("g2c", [128, NC_D], F32)
    be2c = sb("be2c", [128, NC_D], F32)
    bwqc = sb("bwqc", [128, NC_D], F32)
    bwkc = sb("bwkc", [128, NC_D], F32)
    gelb = sb("gelb", [128, NC_F], F32)
    bwv_r = sb("bwv_r", [1, D], BF16)
    b2_r = sb("b2_r", [1, D], BF16)
    bdw_r = sb("bdw_r", [1, D], BF16)
    ones_c = sb("ones_c", [1, 128], BF16)
    projb = sb("projb", [128, 64], BF16)
    maskb = sb("maskb", [128, 128], F32)
    wscol = sb("wscol", [128, 1], F32)
    rws = sb("rws", [128, 1], F32)
    epsc = sb("epsc", [128, 1], F32)
    tinyc = sb("tinyc", [128, 1], F32)
    st6 = sb("st6", [128, 48], F32)
    sc = sb("sc", [128, 72], F32)
    ps = [psb(f"ps{i}", [128, 512]) for i in range(8)]

    sems = {}
    for name in ("ld", "fwa", "fwb", "f8", "cb0", "cb1", "cb2", "cb3",
                 "kc0", "kc1", "kc2", "kc3",
                 "xd0", "xd1", "tdh", "tdq", "tda", "od",
                 "gb0", "gb1", "gb2",
                 "sd", "mm", "vv", "aa", "gg",
                 "swq", "swk", "swv", "swo", "sw1", "sw2", "sw8", "w8l"):
        sems[name] = es.enter_context(nc.semaphore(name))

    p = Prog()
    for n, h in sems.items():
        p.sem(n, h)

    r_ps = [Res() for _ in range(8)]
    r_x = [Res(), Res()]
    r_x2 = Res(); r_sx4 = Res()
    r_htm = Res(); r_hfm = [Res() for _ in range(NTT)]
    r_qkraw = [Res(), Res()]
    r_qp = [Res() for _ in range(NC_D)]
    r_kp = [Res() for _ in range(NC_D)]
    r_tqr = [Res(), Res()]
    r_vpad = [Res() for _ in range(NTT)]
    r_atm = Res(); r_afm = [Res() for _ in range(NTT)]
    r_ktm = [Res() for _ in range(NC_D)]
    r_kvs = [[Res(), Res()] for _ in range(NPAIR)]
    r_ufm = [Res() for _ in range(NC_F)]
    r_qs = [Res() for _ in range(4)]
    r_cb = [Res(), Res(), Res()]
    r_kc = [Res(), Res(), Res(), Res()]
    r_ww = {"wq": r_cb, "wk": r_cb, "w1": r_cb,
            "wv": r_kc, "wo": r_kc, "w2": r_kc, "w8": r_kc}
    r_fw = [Res(), Res()]
    r_fu8 = Res()
    r_sw = {n: Res() for n in ("wqs", "wks", "wvs", "wos", "w1s", "w2s", "w8s")}
    r_stats = Res()
    r_dsc = [Res() for _ in range(NPAIR)]
    r_otmp = [Res(), Res(), Res(), Res()]
    r_rows = Res()

    G = p.group

    def col_load(dst, src_ap, ncols):
        ap = bass.AP(src_ap.tensor, 0, [[1, 128], [128, ncols]])
        return lambda e, dst=dst, ap=ap, n=ncols: e.dma_start(
            out=dst[:, 0:n], in_=ap, allow_slow_non_contiguous=True)

    G("sync", [col_load(g1c, g1_h, NC_D)], writes=[r_rows], sem="ld", dma=True)
    G("sync", [col_load(be1c, be1_h, NC_D)], writes=[r_rows], sem="ld", dma=True)
    G("sync", [col_load(g2c, g2_h, NC_D)], writes=[r_rows], sem="ld", dma=True)
    G("sync", [col_load(be2c, be2_h, NC_D)], writes=[r_rows], sem="ld", dma=True)
    G("sync", [lambda e: e.dma_start(out=maskb[:], in_=mask_h)], writes=[r_rows], sem="ld", dma=True)
    G("sync", [lambda e: e.dma_start(out=wscol[:], in_=wsc_h)], writes=[r_rows], sem="ld", dma=True)
    rowf = fw[0:1, 1024:2048]
    G("sync", [lambda e: e.dma_start(out=rowf, in_=fbias_h)], writes=[r_fw[1]], sem="ld", dma=True)
    G("sync", [lambda e: e.dma_start(out=fw[0:64, 0:64], in_=proj_h)], writes=[r_fw[0]], sem="ld", dma=True)
    G("sync", [lambda e: e.dma_start(out=fw[64:128, 0:64], in_=proj_h)], writes=[r_fw[0]], sem="ld", dma=True)

    G("vector", [lambda e: e.memset(epsc[:], 1e-5)], writes=[r_rows])
    G("vector", [lambda e: e.memset(tinyc[:], 1e-35)], writes=[r_rows])
    G("vector", [lambda e: e.memset(ones_c[:], 1.0)], writes=[r_rows])
    for tt in range(NTT):
        oc = vpad[:, tt * H * 65:(tt + 1) * H * 65].rearrange(
            "p (h y) -> p h y", h=H)[:, :, 64:65]
        G("vector", [lambda e, o=oc: e.memset(o, 1.0)], writes=[r_vpad[tt]])
    G("vector", [lambda e: e.tensor_copy(projb[:], fw[:, 0:64])],
      reads=[r_fw[0]], writes=[r_rows], force=True)
    G("vector", [lambda e: e.reciprocal(rws[:], wscol[:])],
      reads=[r_rows], writes=[r_rows], force=True)
    G("vector", [lambda e: e.tensor_scalar(bdw_r[:], rowf, rws[0:1, :], None,
                                           ALU.mult)],
      reads=[r_fw[1], r_rows], writes=[r_rows], force=True)
    G("sync", [lambda e: e.dma_start(out=rowf, in_=b2_h)],
      writes=[r_fw[1]], sem="ld", dma=True)
    G("vector", [lambda e: e.tensor_copy(b2_r[:], rowf)],
      reads=[r_fw[1]], writes=[r_rows], force=True)

    def prep_qk(w_h, ws_h, bwc, rs_w, sw_sem):
        for k in range(NC_D):
            pk = k % 2
            G("sync", [lambda e, k=k, pk=pk: e.dma_start(
                out=fw[:, pk * 1024:(pk + 1) * 1024],
                in_=w_h[k * 128:(k + 1) * 128, :])],
              writes=[r_fw[pk]], sem=("fwa" if pk == 0 else "fwb"), dma=True)
            G("vector", [lambda e, k=k, pk=pk: e.tensor_scalar(
                htm[:], fw[:, pk * 1024:(pk + 1) * 1024], g1c[:, k:k + 1],
                None, ALU.mult)],
              reads=[r_fw[pk], r_rows], writes=[r_htm])
            fns = []
            for c in range(NC_D):
                fns.append(lambda e, pk=pk, c=c, k=k: e.matmul(
                    ps[6][:, c:c + 1],
                    fw[:, pk * 1024 + c * 128: pk * 1024 + (c + 1) * 128],
                    be1c[:, k:k + 1], start=True, stop=True))
            G("tensor", fns, reads=[r_fw[pk], r_rows], writes=[r_ps[6]])
            if k == 0:
                G("vector", [lambda e: e.tensor_copy(bwc[:, 0:NC_D], ps[6][:, 0:NC_D])],
                  reads=[r_ps[6]], writes=[r_rows], force=True)
            else:
                G("vector", [lambda e: e.tensor_tensor(
                    bwc[:, 0:NC_D], bwc[:, 0:NC_D], ps[6][:, 0:NC_D], ALU.add)],
                  reads=[r_ps[6]], writes=[r_rows], force=True)
            def store(e, k=k):
                src = htm[:, :].rearrange("p (c x) -> p c x", c=NC_D)
                dst = bass.AP(ws_h.tensor, k * (128 * 128 * NC_D) // NC_D * 1,
                              [[128, 128], [NC_D * 128 * 128, NC_D], [1, 128]])
                dst = bass.AP(ws_h.tensor, k * 128 * 128,
                              [[128, 128], [NC_D * 128 * 128, NC_D], [1, 128]])
                return e.dma_start(out=dst, in_=src)
            G("sync", [store], reads=[r_htm], writes=[rs_w], sem=sw_sem, dma=True)

    prep_qk(wq_h, wqs_h, bwqc, r_sw["wqs"], "swq")
    prep_qk(wk_h, wks_h, bwkc, r_sw["wks"], "swk")

    for k in range(NC_D):
        pk = k % 2
        G("sync", [lambda e, k=k, pk=pk: e.dma_start(
            out=fw[:, pk * 1024:(pk + 1) * 1024],
            in_=wv_h[k * 128:(k + 1) * 128, :])],
          writes=[r_fw[pk]], sem=("fwa" if pk == 0 else "fwb"), dma=True)
        G("vector", [lambda e, k=k, pk=pk: e.tensor_scalar(
            htm[:], fw[:, pk * 1024:(pk + 1) * 1024], g1c[:, k:k + 1],
            None, ALU.mult)],
          reads=[r_fw[pk], r_rows], writes=[r_htm])
        fns = []
        for half in range(2):
            fns.append(lambda e, k=k, pk=pk, half=half: e.matmul(
                ps[6 + half][0:1, 0:512],
                be1c[:, k:k + 1],
                fw[:, pk * 1024 + half * 512: pk * 1024 + (half + 1) * 512],
                start=(k == 0), stop=(k == NC_D - 1)))
        G("tensor", fns, reads=[r_fw[pk], r_rows], writes=[r_ps[6], r_ps[7]])
        G("sync", [lambda e, k=k: e.dma_start(out=wvs_h[k], in_=htm[:])],
          reads=[r_htm], writes=[r_sw["wvs"]], sem="swv", dma=True)
    G("vector", [lambda e: e.tensor_copy(bwv_r[:, 0:512], ps[6][0:1, 0:512])],
      reads=[r_ps[6]], writes=[r_rows], force=True)
    G("vector", [lambda e: e.tensor_copy(bwv_r[:, 512:1024], ps[7][0:1, 0:512])],
      reads=[r_ps[7]], writes=[r_rows], force=True)

    for k in range(NC_D):
        pk = k % 2
        G("sync", [lambda e, k=k, pk=pk: e.dma_start(
            out=fw[:, pk * 1024:(pk + 1) * 1024],
            in_=wo_h[k * 128:(k + 1) * 128, :])],
          writes=[r_fw[pk]], sem=("fwa" if pk == 0 else "fwb"), dma=True)
        G("vector", [lambda e, pk=pk: e.tensor_copy(
            htm[:], fw[:, pk * 1024:(pk + 1) * 1024])],
          reads=[r_fw[pk]], writes=[r_htm])
        G("sync", [lambda e, k=k: e.dma_start(out=wos_h[k], in_=htm[:])],
          reads=[r_htm], writes=[r_sw["wos"]], sem="swo", dma=True)

    G("sync", [col_load(gelb, b1_h, NC_F)], writes=[r_rows], sem="ld", dma=True)
    for k in range(NC_D):
        for q in range(4):
            pk = (k * 4 + q) % 2
            G("sync", [lambda e, k=k, q=q, pk=pk: e.dma_start(
                out=fw[:, pk * 1024:(pk + 1) * 1024],
                in_=w1_h[k * 128:(k + 1) * 128, q * 1024:(q + 1) * 1024])],
              writes=[r_fw[pk]], sem=("fwa" if pk == 0 else "fwb"), dma=True)
            G("vector", [lambda e, k=k, pk=pk: e.tensor_scalar(
                htm[:], fw[:, pk * 1024:(pk + 1) * 1024], g2c[:, k:k + 1],
                None, ALU.mult)],
              reads=[r_fw[pk], r_rows], writes=[r_htm])
            fns = []
            for c in range(8):
                fns.append(lambda e, pk=pk, c=c, k=k: e.matmul(
                    ps[7][:, c:c + 1],
                    fw[:, pk * 1024 + c * 128: pk * 1024 + (c + 1) * 128],
                    be2c[:, k:k + 1], start=True, stop=True))
            G("tensor", fns, reads=[r_fw[pk], r_rows], writes=[r_ps[7]])
            G("vector", [lambda e, q=q: e.tensor_tensor(
                gelb[:, q * 8:(q + 1) * 8], gelb[:, q * 8:(q + 1) * 8],
                ps[7][:, 0:8], ALU.add)],
              reads=[r_ps[7]], writes=[r_rows], force=True)
            def store1(e, k=k, q=q):
                src = htm[:, :].rearrange("p (c x) -> p c x", c=8)
                dst = bass.AP(w1s_h.tensor,
                              q * 8 * (NC_D * 128 * 128) + k * 128 * 128,
                              [[128, 128], [NC_D * 128 * 128, 8], [1, 128]])
                return e.dma_start(out=dst, in_=src)
            G("sync", [store1], reads=[r_htm], writes=[r_sw["w1s"]], sem="sw1", dma=True)
    for k in range(NC_F):
        pk = k % 2
        G("sync", [lambda e, k=k, pk=pk: e.dma_start(
            out=fw[:, pk * 1024:(pk + 1) * 1024],
            in_=w2_h[k * 128:(k + 1) * 128, :])],
          writes=[r_fw[pk]], sem=("fwa" if pk == 0 else "fwb"), dma=True)
        G("vector", [lambda e, pk=pk: e.tensor_copy(
            htm[:], fw[:, pk * 1024:(pk + 1) * 1024])],
          reads=[r_fw[pk]], writes=[r_htm])
        G("sync", [lambda e, k=k: e.dma_start(out=w2s_h[k], in_=htm[:])],
          reads=[r_htm], writes=[r_sw["w2s"]], sem="sw2", dma=True)

    for k in range(NC_D):
        u = fw[:, 0:1024]
        w0 = hfm[:, 0:1024]
        w1_ = hfm[:, 1024:2048]
        w2_ = hfm[:, 2048:3072]
        w3_ = hfm[:, 3072:4096]
        for hf in range(2):
            G("sync", [lambda e, k=k, hf=hf: e.dma_start(
                out=fu8[:], in_=w8_h[k * 128:(k + 1) * 128, hf * 512:(hf + 1) * 512])],
              writes=[r_fu8], sem="f8", dma=True)
            G("vector", [lambda e, hf=hf: e.tensor_copy(
                u[:, hf * 512:(hf + 1) * 512], fu8[:])],
              reads=[r_fu8], writes=[r_fw[0]])
        fns_v = [
            lambda e: e.tensor_scalar(w0, u, 128.0, 128.0, ALU.is_ge, ALU.mult),
            lambda e: e.tensor_tensor(w1_, u, w0, ALU.subtract),
            lambda e: e.tensor_scalar(w2_, w1_, 64.0, 64.0, ALU.is_ge, ALU.mult),
            lambda e: e.tensor_tensor(w1_, w1_, w2_, ALU.subtract),
            lambda e: e.tensor_scalar(w3_, w1_, 32.0, 32.0, ALU.is_ge, ALU.mult),
            lambda e: e.tensor_tensor(w1_, w1_, w3_, ALU.subtract),
            lambda e: e.tensor_tensor(w2_, w2_, w3_, ALU.add),
            lambda e: e.tensor_scalar(w3_, w1_, 16.0, 16.0, ALU.is_ge, ALU.mult),
            lambda e: e.tensor_tensor(w1_, w1_, w3_, ALU.subtract),
            lambda e: e.tensor_tensor(w2_, w2_, w3_, ALU.add),
            lambda e: e.tensor_scalar(w3_, w1_, 8.0, 8.0, ALU.is_ge, ALU.mult),
            lambda e: e.tensor_tensor(w1_, w1_, w3_, ALU.subtract),
            lambda e: e.tensor_tensor(w2_, w2_, w3_, ALU.add),
            lambda e: e.tensor_scalar(w3_, w2_, 8.0, 8.0, ALU.is_ge, ALU.mult),
            lambda e: e.tensor_tensor(w1_, w1_, w3_, ALU.add),
            lambda e: e.tensor_scalar(w2_, w2_, 8.0, -80.0, ALU.max, ALU.add),
        ]
        G("vector", fns_v, reads=[r_fu8], writes=[r_hfm[0], r_fw[0]])
        G("scalar", [lambda e, w2_=w2_: e.activation(
            w2_, w2_, AF.Exp, scale=LN2 / 8.0)],
          reads=[r_hfm[0]], writes=[r_hfm[0]])
        fns_v2 = [
            lambda e: e.tensor_tensor(w1_, w1_, w2_, ALU.mult),
            lambda e: e.tensor_scalar(w0, w0, -1.0 / 64.0, 1.0, ALU.mult, ALU.add),
            lambda e: e.tensor_tensor(w3_, w1_, w0, ALU.mult),
        ]
        G("vector", fns_v2, reads=[r_hfm[0]], writes=[r_hfm[0]])
        G("sync", [lambda e, k=k, w3_=w3_: e.dma_start(out=w8s_h[k], in_=w3_)],
          reads=[r_hfm[0]], writes=[r_sw["w8s"]], sem="sw8", dma=True)

    def load_cblocks(ws4_h, rs, c, slot_r, window, pk, q="sync"):
        def f(e, c=c, pk=pk):
            dst = window[:, pk * NC_D * 128:(pk + 1) * NC_D * 128].rearrange(
                "p (k x) -> p k x", k=NC_D)
            srcap = bass.AP(ws4_h.tensor, c * NC_D * 128 * 128,
                            [[128, 128], [128 * 128, NC_D], [1, 128]])
            return e.dma_start(out=dst, in_=srcap)
        sem = ("gb%d" if q == "gpsimd" else "cb%d") % pk
        G(q, [f], reads=[rs], writes=[slot_r], sem=sem, dma=True)

    def load_kchunk(ws_h, rs, k, slot_r, window, pk):
        def f(e, k=k, pk=pk):
            return e.dma_start(out=window[:, pk * D:(pk + 1) * D], in_=ws_h[k])
        G("sync", [f], reads=[rs], writes=[slot_r],
          sem="kc%d" % pk, dma=True)

    def load_x(s, buf):
        def f(e, s=s, buf=buf):
            src = bass.AP(x_h.tensor, s * T * D,
                          [[D, 128], [128 * D, NTT], [1, D]])
            return e.dma_start(out=x_ab[:, buf * NTT * D:(buf + 1) * NTT * D]
                               .rearrange("p (tt d) -> p tt d", tt=NTT),
                               in_=src)
        G("gpsimd", [f], writes=[r_x[buf]], sem=("xd0" if buf == 0 else "xd1"), dma=True)

    load_x(0, 0)

    r_sd8 = [Res() for _ in range(8)]

    def ln_stats_slab(tile_of, reads, i_m, i_a):
        for tt in range(NTT):
            ta = tile_of(tt)
            G("vector", [lambda e, ta=ta, tt=tt: e.tensor_scalar(
                qs[:, 0:D], ta, 0.0, None, ALU.add, ALU.add,
                accum_out=sc[:, 32 + tt:33 + tt])],
              reads=list(reads), writes=[r_sd8[2 * tt]], force=True)
            G("vector", [lambda e, ta=ta, tt=tt: e.scalar_tensor_tensor(
                qs[:, D:2 * D], ta, 1.0, ta, ALU.mult, ALU.mult,
                accum_out=sc[:, 36 + tt:37 + tt])],
              reads=list(reads), writes=[r_sd8[2 * tt + 1]], force=True)
        G("vector", [lambda e, i_m=i_m: e.tensor_scalar(
            sc[:, i_m:i_m + 4], sc[:, 32:36], 1.0 / D, None, ALU.mult)],
          reads=list(r_sd8), writes=[r_stats], force=True)
        G("vector", [lambda e, i_m=i_m: e.tensor_tensor(
            sc[:, 40:44], sc[:, i_m:i_m + 4], sc[:, i_m:i_m + 4], ALU.mult)],
          writes=[r_stats], force=True)
        G("vector", [lambda e: e.scalar_tensor_tensor(
            sc[:, 44:48], sc[:, 36:40], 1.0 / D, sc[:, 40:44],
            ALU.mult, ALU.subtract)],
          writes=[r_stats], force=True)
        G("scalar", [lambda e: e.activation(
            sc[:, 56:60], sc[:, 44:48], AF.Ln, bias=epsc[:])],
          reads=[r_rows], writes=[r_stats], force=True)
        G("scalar", [lambda e, i_a=i_a: e.activation(
            sc[:, i_a:i_a + 4], sc[:, 56:60], AF.Exp, scale=-0.5)],
          writes=[r_stats], force=True)

    def phase_A(s):
        xb = s % 2
        xoff = xb * NTT * D
        ln_stats_slab(lambda tt: x_ab[:, xoff + tt * D: xoff + (tt + 1) * D],
                      [r_x[xb]], 0, 4)
        if DBG and s == 0:
            G("sync", [lambda e: e.dma_start(out=dbg["st6"], in_=st6[:, 0:12])],
              reads=[r_stats], sem="od", dma=True)
            G("sync", [lambda e: e.dma_start(out=dbg["scc"][:, 0:8], in_=sc[:, 0:8])],
              reads=[r_stats], sem="od", dma=True)
        for tt in range(NTT):
            xt = x_ab[:, xoff + tt * D: xoff + (tt + 1) * D]
            G("vector", [lambda e, xt=xt, tt=tt: e.tensor_scalar(
                htm[:], xt, sc[:, 0 + tt:1 + tt], sc[:, 4 + tt:5 + tt],
                ALU.subtract, ALU.mult)],
              reads=[r_x[xb], r_stats], writes=[r_htm])
            if DBG and s == 0 and tt == 0:
                G("sync", [lambda e: e.dma_start(out=dbg["htm1"], in_=htm[:])],
                  reads=[r_htm], sem="od", dma=True)
            def tra(e, tt=tt):
                src = htm[:, :].rearrange("p (c x) -> p c x", c=NC_D)
                dst = bass.AP(hfm, tt * 128, [[NC_D * T, 128], [T, NC_D], [1, 128]])
                return e.dma_start(out=dst, in_=src, transpose=True)
            G("scalar", [tra], reads=[r_htm], writes=[r_hfm[tt]], sem="tdh", dma=True)

    phase_A(0)
    for s in range(NSLAB):
        xb = s % 2
        xoff = xb * NTT * D
        if s + 1 < NSLAB:
            load_x(s + 1, (s + 1) % 2)

        if DBG and s == 0:
            G("sync", [lambda e: e.dma_start(out=dbg["hfm"], in_=hfm[:])],
              reads=list(r_hfm), sem="od", dma=True)
        vgrp = [(tt, nh) for tt in range(NTT) for nh in range(2)]
        for k in range(NC_D):
            pk = k % 4
            load_kchunk(wvs_h, r_sw["wvs"], k, r_ww["wv"][pk], wv_w, pk)
            for gi, (tt, nh) in enumerate(vgrp):
                bank = gi
                fns = [lambda e, k=k, pk=pk, tt=tt, nh=nh, bank=bank: e.matmul(
                    ps[bank][:],
                    hfm[:, k * T + tt * 128: k * T + (tt + 1) * 128],
                    wv_w[:, pk * D + nh * 512: pk * D + (nh + 1) * 512],
                    start=(k == 0), stop=False)]
                if k == NC_D - 1:
                    fns.append(lambda e, nh=nh, bank=bank: e.matmul(
                        ps[bank][:], ones_c[:],
                        bwv_r[:, nh * 512:(nh + 1) * 512],
                        start=False, stop=True))
                G("tensor", fns, reads=[r_hfm[tt], r_ww["wv"][pk], r_rows],
                  writes=[r_ps[bank]])
        for gi, (tt, nh) in enumerate(vgrp):
            bank = gi
            def vevac(e, tt=tt, nh=nh, bank=bank):
                src = ps[bank][:, :].rearrange("p (h y) -> p h y", h=8)
                base = tt * H * 65 + nh * 8 * 65
                dst = vpad[:, base: base + 8 * 65].rearrange(
                    "p (h y) -> p h y", h=8)[:, :, 0:64]
                return e.tensor_copy(dst, src)
            G("vector", [vevac], reads=[r_ps[bank]], writes=[r_vpad[tt]])

        for wname, ws4, bwc, rdst, dsttile, window in (
                ("wq", wqs_h, bwqc, r_qp, qp, wq_w),
                ("wk", wks_h, bwkc, r_kp, kp, wk_w)):
            def stage1(c, wname=wname, ws4=ws4, window=window):
                pk = c % 3
                load_cblocks(ws4, r_sw["wqs" if wname == "wq" else "wks"], c,
                             r_ww[wname][pk], window, pk)
                bank = 4 + (c % 2)
                fns = []
                for k in range(NC_D):
                    fns.append(lambda e, k=k, pk=pk, bank=bank, window=window: e.matmul(
                        ps[bank][:],
                        window[:, (pk * NC_D + k) * 128:(pk * NC_D + k + 1) * 128],
                        hfm[:, k * T:(k + 1) * T],
                        start=(k == 0), stop=(k == NC_D - 1)))
                G("tensor", fns, reads=list(r_hfm) + [r_ww[wname][pk]], writes=[r_ps[bank]])
                slot = c % 2
                G("scalar", [lambda e, bank=bank, slot=slot, bwc=bwc, c=c:
                             e.activation(qkraw[:, slot * T:(slot + 1) * T],
                                          ps[bank][:], AF.Identity,
                                          bias=bwc[:, c:c + 1])],
                  reads=[r_ps[bank], r_rows], writes=[r_qkraw[slot]])
            def stage2(c, wname=wname, rdst=rdst, dsttile=dsttile):
                slot = c % 2
                fns2 = [
                    lambda e, slot=slot: e.matmul(
                        ps[6][0:64, :], projb[0:64, :],
                        qkraw[0:64, slot * T:(slot + 1) * T], start=True, stop=True),
                    lambda e, slot=slot: e.matmul(
                        ps[6][64:128, :], projb[64:128, :],
                        qkraw[64:128, slot * T:(slot + 1) * T], start=True, stop=True),
                ]
                G("tensor", fns2, reads=[r_qkraw[slot], r_rows], writes=[r_ps[6]])
                fns3 = [
                    lambda e: e.activation(tqr[:, 0:T], ps[6][:], AF.Exp),
                    lambda e: e.activation(tqr[:, T:2 * T], ps[6][:], AF.Relu),
                ]
                G("scalar", fns3, reads=[r_ps[6]], writes=[r_tqr[0], r_tqr[1]])
                G("vector", [lambda e, c=c, dsttile=dsttile: e.scalar_tensor_tensor(
                    dsttile[:, c * T:(c + 1) * T], tqr[:, 0:T], 1.0,
                    tqr[:, T:2 * T], ALU.min, ALU.add)],
                  reads=[r_tqr[0], r_tqr[1]], writes=[rdst[c]])
                if wname == "wk":
                    def trk(e, c=c):
                        srcT = kp[:, c * T:(c + 1) * T].rearrange("p (t x) -> p t x", t=NTT)
                        dstT = bass.AP(ktmb, c * 128, [[NTT * D, 128], [D, NTT], [1, 128]])
                        return e.dma_start(out=dstT, in_=srcT, transpose=True)
                    G("scalar", [trk], reads=[r_kp[c]], writes=[r_ktm[c]],
                      sem="tdh", dma=True)
            for c in range(NC_D):
                stage1(c)
                if c > 0:
                    stage2(c - 1)
            stage2(NC_D - 1)

        if DBG and s == 0:
            G("sync", [lambda e: e.dma_start(out=dbg["qp"], in_=qp[:])], reads=[r_qp[7]], sem="od", dma=True)
            G("sync", [lambda e: e.dma_start(out=dbg["kp"], in_=kp[:])], reads=[r_kp[7]], sem="od", dma=True)
            G("sync", [lambda e: e.dma_start(out=dbg["vpad"], in_=vpad[:])], reads=[r_vpad[3]], sem="od", dma=True)
            G("sync", [lambda e: e.dma_start(out=dbg["ktm"], in_=ktmb[:])], reads=list(r_ktm), sem="od", dma=True)
        ktm = ktmb
        for ch in range(NTT):
            seq_ch = (s * NTT + ch) % (4 * NTT)
            par_r = seq_ch % 2
            par_w = (seq_ch + 1) % 2

            def stage1(pr, ch=ch):
                c = pr
                qoff = c * T + ch * 128
                se = 0 if pr % 2 == 0 else 6
                so = 1 if pr % 2 == 0 else 7
                G("tensor", [lambda e, qoff=qoff, se=se: e.matmul(
                    ps[se][:, 0:128],
                    kp[0:64, qoff:qoff + 128], qp[0:64, qoff:qoff + 128],
                    start=True, stop=True)],
                  reads=[r_kp[c], r_qp[c]], writes=[r_ps[se]])
                G("tensor", [lambda e, qoff=qoff, so=so: e.matmul(
                    ps[so][:, 0:128],
                    kp[64:128, qoff:qoff + 128], qp[64:128, qoff:qoff + 128],
                    start=True, stop=True)],
                  reads=[r_kp[c], r_qp[c]], writes=[r_ps[so]])
                sb_off = (pr % 2) * 512
                ssb = tqr[:, sb_off:sb_off + 256]
                G("vector", [
                    lambda e, ssb=ssb, se=se: e.tensor_tensor(
                        ssb[:, 0:128], ps[se][:, 0:128], maskb[:, 0:128], ALU.mult),
                    lambda e, ssb=ssb, so=so: e.tensor_tensor(
                        ssb[:, 128:256], ps[so][:, 0:128], maskb[:, 0:128], ALU.mult)],
                  reads=[r_ps[se], r_ps[so], r_rows], writes=[r_tqr[pr % 2]])

            def stage2(pr, ch=ch, seq_ch=seq_ch, par_r=par_r, par_w=par_w):
                c = pr
                qoff = c * T + ch * 128
                sb_off = (pr % 2) * 512
                ssb = tqr[:, sb_off:sb_off + 256]
                kv_r = pr * 130 + par_r * 65
                kv_w = pr * 130 + par_w * 65
                fns = [lambda e, ssb=ssb, ch=ch, pr=pr, st=(seq_ch == 0): e.matmul(
                    ps[2][:, 0:65], ssb[:, 0:128],
                    vpad[:, ch * H * 65 + (pr * 2) * 65: ch * H * 65 + (pr * 2) * 65 + 65],
                    start=True, stop=st)]
                if seq_ch != 0:
                    fns.append(lambda e, qoff=qoff, kv_r=kv_r: e.matmul(
                        ps[2][:, 0:65], qp[0:64, qoff:qoff + 128],
                        kvb[0:64, kv_r:kv_r + 65], start=False, stop=True))
                rd = [r_tqr[pr % 2], r_vpad[ch], r_qp[c]]
                if seq_ch != 0:
                    rd.append(r_kvs[pr][par_r])
                G("tensor", fns, reads=rd, writes=[r_ps[2]])
                G("tensor", [lambda e, ssb=ssb, ch=ch, pr=pr: e.matmul(
                    ps[3][:, 0:65], ssb[:, 128:256],
                    vpad[:, ch * H * 65 + (pr * 2 + 1) * 65: ch * H * 65 + (pr * 2 + 1) * 65 + 65],
                    start=True, stop=True)],
                  reads=[r_tqr[pr % 2], r_vpad[ch]], writes=[r_ps[3]])
                if seq_ch != 0:
                    G("tensor", [lambda e, qoff=qoff, kv_r=kv_r: e.matmul(
                        ps[4][:, 0:65], qp[64:128, qoff:qoff + 128],
                        kvb[64:128, kv_r:kv_r + 65], start=True, stop=True)],
                      reads=[r_qp[c], r_kvs[pr][par_r]], writes=[r_ps[4]])
                eslot = otmp[:, (pr % 2) * 65:(pr % 2) * 65 + 65]
                oslot = otmp[:, (2 + pr % 2) * 65:(2 + pr % 2) * 65 + 65]
                G("vector", [lambda e, eslot=eslot: e.tensor_copy(eslot, ps[2][:, 0:65])],
                  reads=[r_ps[2]], writes=[r_otmp[pr % 2]], force=True)
                if seq_ch != 0:
                    G("vector", [lambda e, oslot=oslot: e.tensor_copy(oslot, ps[4][:, 0:65])],
                      reads=[r_ps[4]], writes=[r_otmp[2 + pr % 2]], force=True)
                    G("vector", [lambda e, oslot=oslot: e.tensor_tensor(
                        oslot, oslot, ps[3][:, 0:65], ALU.add)],
                      reads=[r_ps[3]], writes=[r_otmp[2 + pr % 2]], force=True)
                else:
                    G("vector", [lambda e, oslot=oslot: e.tensor_copy(oslot, ps[3][:, 0:65])],
                      reads=[r_ps[3]], writes=[r_otmp[2 + pr % 2]], force=True)
                dn = dsc[:, pr * 4:pr * 4 + 2]
                rdn = dsc[:, pr * 4 + 2:pr * 4 + 4]
                G("vector", [
                    lambda e, dn=dn, eslot=eslot: e.tensor_scalar(
                        dn[:, 0:1], eslot[:, 64:65], 1e-6, None, ALU.add),
                    lambda e, dn=dn, oslot=oslot: e.tensor_scalar(
                        dn[:, 1:2], oslot[:, 64:65], 1e-6, None, ALU.add)],
                  reads=[r_otmp[pr % 2], r_otmp[2 + pr % 2]],
                  writes=[r_dsc[pr]], force=True)
                G("vector", [lambda e, rdn=rdn, dn=dn: e.reciprocal(rdn, dn)],
                  writes=[r_dsc[pr]], force=True)
                head_e = pr * 2
                G("scalar", [
                    lambda e, head_e=head_e, rdn=rdn, eslot=eslot: e.activation(
                        atm[:, head_e * 64:(head_e + 1) * 64],
                        eslot[:, 0:64], AF.Identity, scale=rdn[:, 0:1]),
                    lambda e, head_e=head_e, rdn=rdn, oslot=oslot: e.activation(
                        atm[:, (head_e + 1) * 64:(head_e + 2) * 64],
                        oslot[:, 0:64], AF.Identity, scale=rdn[:, 1:2])],
                  reads=[r_otmp[pr % 2], r_otmp[2 + pr % 2], r_dsc[pr]],
                  writes=[r_atm])
                fns = []
                for hh in range(2):
                    koff = ch * D + c * 128 + hh * 64
                    vb = ch * H * 65 + (pr * 2 + hh) * 65
                    fns.append(lambda e, koff=koff, vb=vb, hh=hh: e.matmul(
                        ps[5][hh * 64:(hh + 1) * 64, 0:65],
                        ktm[:, koff:koff + 64],
                        vpad[:, vb:vb + 65], start=True, stop=True))
                G("tensor", fns, reads=[r_ktm[c], r_vpad[ch]], writes=[r_ps[5]])
                dst = kvs[:, kv_w:kv_w + 65]
                dstb = kvb[:, kv_w:kv_w + 65]
                srcp = ps[5][:, 0:65]
                old_ = kvs[:, kv_r:kv_r + 65]
                rd = [r_ps[5]] + ([r_kvs[pr][par_r]] if seq_ch != 0 else [])
                if seq_ch == 0:
                    G("vector", [lambda e, dst=dst, srcp=srcp: e.tensor_copy(dst, srcp)],
                      reads=rd, writes=[r_kvs[pr][par_w]], force=True)
                else:
                    G("vector", [lambda e, dst=dst, srcp=srcp, old_=old_:
                                 e.tensor_tensor(dst, old_, srcp, ALU.add)],
                      reads=rd, writes=[r_kvs[pr][par_w]], force=True)
                G("vector", [lambda e, dst=dst, dstb=dstb: e.tensor_copy(dstb, dst)],
                  reads=[r_kvs[pr][par_w]], writes=[r_kvs[pr][par_w]], force=True)

            for pr in range(NPAIR):
                stage1(pr)
                if pr > 0:
                    stage2(pr - 1)
            stage2(NPAIR - 1)
            def trat(e, ch=ch):
                src = atm[:, :].rearrange("p (c x) -> p c x", c=NC_D)
                dst = bass.AP(afm, ch * 128, [[NC_D * T, 128], [T, NC_D], [1, 128]])
                return e.dma_start(out=dst, in_=src, transpose=True)
            G("scalar", [trat], reads=[r_atm], writes=[r_afm[ch]], sem="tda", dma=True)

        dgrp = [(tt, nh) for tt in range(NTT) for nh in range(2)]
        for k in range(NC_D):
            pk = k % 4
            load_kchunk(wos_h, r_sw["wos"], k, r_ww["wo"][pk], wo_w, pk)
            for gi, (tt, nh) in enumerate(dgrp):
                bank = gi
                G("tensor", [lambda e, k=k, pk=pk, tt=tt, nh=nh, bank=bank: e.matmul(
                    ps[bank][:],
                    afm[:, k * T + tt * 128: k * T + (tt + 1) * 128],
                    wo_w[:, pk * D + nh * 512: pk * D + (nh + 1) * 512],
                    start=(k == 0), stop=(k == NC_D - 1))],
                  reads=[r_afm[tt], r_ww["wo"][pk]], writes=[r_ps[bank]])
        for gi, (tt, nh) in enumerate(dgrp):
            bank = gi
            G("vector", [lambda e, tt=tt, nh=nh, bank=bank, xoff=xoff:
                         e.scalar_tensor_tensor(
                x2b[:, tt * D + nh * 512: tt * D + (nh + 1) * 512],
                ps[bank][:], 1.0,
                x_ab[:, xoff + tt * D + nh * 512: xoff + tt * D + (nh + 1) * 512],
                ALU.mult, ALU.add)],
              reads=[r_ps[bank], r_x[xb]], writes=[r_x2])

        if DBG and s == 0:
            G("sync", [lambda e: e.dma_start(out=dbg["x2"], in_=x2b[:])], reads=[r_x2], sem="od", dma=True)
        ln_stats_slab(lambda tt: x2b[:, tt * D:(tt + 1) * D], [r_x2], 8, 12)
        for tt in range(NTT):
            x2t = x2b[:, tt * D:(tt + 1) * D]
            G("vector", [lambda e, x2t=x2t, tt=tt: e.tensor_scalar(
                htm[:], x2t, sc[:, 8 + tt:9 + tt], sc[:, 12 + tt:13 + tt],
                ALU.subtract, ALU.mult)],
              reads=[r_x2, r_stats], writes=[r_htm])
            def trh2(e, tt=tt):
                src = htm[:, :].rearrange("p (c x) -> p c x", c=NC_D)
                dst = bass.AP(kp, tt * 128, [[NC_D * T, 128], [T, NC_D], [1, 128]])
                return e.dma_start(out=dst, in_=src, transpose=True)
            G("scalar", [trh2], reads=[r_htm], writes=list(r_kp), sem="tdq", dma=True)
            G("gpsimd", [lambda e, tt=tt, xoff=xoff: e.tensor_tensor(
                sx4[:, tt * D:(tt + 1) * D],
                x_ab[:, xoff + tt * D: xoff + (tt + 1) * D],
                x2b[:, tt * D:(tt + 1) * D], ALU.add)],
              reads=[r_x[xb], r_x2], writes=[r_sx4])

        h2fm = kp
        for fc in range(NC_F):
            pk = fc % 3
            load_cblocks(w1s_h, r_sw["w1s"], fc, r_ww["w1"][pk], w1_w, pk, q="gpsimd")
            bank = 4 + (fc % 2)
            fns = []
            for k in range(NC_D):
                fns.append(lambda e, k=k, pk=pk, bank=bank: e.matmul(
                    ps[bank][:],
                    w1_w[:, (pk * NC_D + k) * 128:(pk * NC_D + k + 1) * 128],
                    h2fm[:, k * T:(k + 1) * T],
                    start=(k == 0), stop=(k == NC_D - 1)))
            G("tensor", fns, reads=list(r_kp) + [r_ww["w1"][pk]], writes=[r_ps[bank]])
            G("scalar", [lambda e, fc=fc, bank=bank: e.activation(
                ufm[:, fc * T:(fc + 1) * T], ps[bank][:], AF.Gelu,
                bias=gelb[:, fc:fc + 1])],
              reads=[r_ps[bank], r_rows], writes=[r_ufm[fc]])
        for fc in range(NC_F):
            pk = fc % 4
            load_kchunk(w2s_h, r_sw["w2s"], fc, r_ww["w2"][pk], w2_w, pk)
            for gi, (tt, nh) in enumerate([(t_, n_) for t_ in range(NTT) for n_ in range(2)]):
                bank = gi
                fns = [lambda e, fc=fc, pk=pk, tt=tt, nh=nh, bank=bank: e.matmul(
                    ps[bank][:],
                    ufm[:, fc * T + tt * 128: fc * T + (tt + 1) * 128],
                    w2_w[:, pk * D + nh * 512: pk * D + (nh + 1) * 512],
                    start=(fc == 0), stop=False)]
                if fc == NC_F - 1:
                    fns.append(lambda e, nh=nh, bank=bank: e.matmul(
                        ps[bank][:], ones_c[:],
                        b2_r[:, nh * 512:(nh + 1) * 512],
                        start=False, stop=True))
                G("tensor", fns,
                  reads=[r_ufm[fc], r_ww["w2"][pk], r_rows],
                  writes=[r_ps[bank]])
        for gi, (tt, nh) in enumerate([(t_, n_) for t_ in range(NTT) for n_ in range(2)]):
            bank = gi
            G("vector", [lambda e, tt=tt, nh=nh, bank=bank: e.scalar_tensor_tensor(
                x2b[:, tt * D + nh * 512: tt * D + (nh + 1) * 512],
                ps[bank][:], 1.0,
                sx4[:, tt * D + nh * 512: tt * D + (nh + 1) * 512],
                ALU.mult, ALU.add)],
              reads=[r_ps[bank], r_sx4], writes=[r_x2])

        for k in range(NC_D):
            def f8l(e, k=k):
                return e.dma_start(out=ufm[:, k * 1024:(k + 1) * 1024],
                                   in_=w8s_h[k])
            G("sync", [f8l], reads=[r_sw["w8s"]],
              writes=[r_ufm[2 * k], r_ufm[2 * k + 1]], sem="w8l", dma=True)

        ln_stats_slab(lambda tt: x2b[:, tt * D:(tt + 1) * D], [r_x2], 16, 20)
        for tt in range(NTT):
            zt = x2b[:, tt * D:(tt + 1) * D]
            x4t = sx4[:, tt * D:(tt + 1) * D]
            G("vector", [lambda e, zt=zt, x4t=x4t, tt=tt: e.tensor_scalar(
                x4t, zt, sc[:, 16 + tt:17 + tt], sc[:, 20 + tt:21 + tt],
                ALU.subtract, ALU.mult)],
              reads=[r_x2, r_stats], writes=[r_sx4])
            q0 = qs[:, 0:D]; q1 = qs[:, D:2 * D]
            q2 = qs[:, 2 * D:3 * D]; q3 = qs[:, 3 * D:4 * D]
            lf = x2b[:, tt * D:(tt + 1) * D]
            G("vector", [lambda e, x4t=x4t: e.scalar_tensor_tensor(
                q0, x4t, -1.0, x4t, ALU.mult, ALU.max)],
              reads=[r_sx4], writes=[r_qs[0]])
            G("vector", [lambda e: e.tensor_scalar(
                q1, q0, 1e-30, None, ALU.max, ALU.max, accum_out=sc[:, 68:69])],
              reads=[r_qs[0]], writes=[r_qs[1], r_stats], force=True)
            G("vector", [lambda e: e.tensor_scalar(
                sc[:, 69:70], sc[:, 68:69], 1e-12, 1.0 / 448.0, ALU.max, ALU.mult)],
              writes=[r_stats], force=True)
            fns_a = [
                lambda e: e.activation(sc[:, 70:71], sc[:, 69:70], AF.Ln,
                                       bias=tinyc[:], scale=2.0 ** -6),
                lambda e: e.activation(sc[:, 71:72], sc[:, 69:70], AF.Ln,
                                       bias=tinyc[:], scale=2.0 ** -9),
                lambda e, lf=lf: e.activation(lf, q1, AF.Ln, bias=tinyc[:]),
            ]
            G("scalar", fns_a, reads=[r_qs[1], r_rows, r_x2],
              writes=[r_stats, r_x2], force=True)
            G("vector", [lambda e, lf=lf: e.tensor_scalar(
                q1, lf, sc[:, 70:71], INV_LN2, ALU.subtract, ALU.mult)],
              reads=[r_x2, r_stats], writes=[r_qs[1]])
            for w in (8.0, 4.0, 2.0, 1.0):
                G("vector", [lambda e, w=w: e.tensor_scalar(
                    q2, q1, w, w, ALU.is_ge, ALU.mult)],
                  reads=[r_qs[1]], writes=[r_qs[2]])
                G("vector", [lambda e: e.tensor_tensor(q1, q1, q2, ALU.subtract)],
                  reads=[r_qs[2]], writes=[r_qs[1]])
            G("vector", [lambda e, lf=lf: e.tensor_scalar(
                q3, lf, sc[:, 70:71], INV_LN2, ALU.subtract, ALU.mult)],
              reads=[r_x2, r_stats], writes=[r_qs[3]])
            G("vector", [lambda e: e.tensor_tensor(q3, q3, q1, ALU.subtract)],
              reads=[r_qs[1]], writes=[r_qs[3]])
            fns_a = [
                lambda e: e.activation(q2, q1, AF.Exp, scale=LN2),
                lambda e: e.activation(q3, q3, AF.Exp, bias=sc[:, 71:72], scale=LN2),
            ]
            G("scalar", fns_a, reads=[r_qs[1], r_qs[3], r_stats],
              writes=[r_qs[2], r_qs[3]])
            G("vector", [lambda e: e.tensor_scalar(
                q0, q2, 8.0, 0.5, ALU.mult, ALU.add)],
              reads=[r_qs[2]], writes=[r_qs[0]])
            G("vector", [lambda e: e.tensor_copy(q1, q0)],
              reads=[r_qs[0]], writes=[r_qs[1]])
            for w in (16.0, 8.0, 4.0, 2.0, 1.0):
                G("vector", [lambda e, w=w: e.tensor_scalar(
                    q2, q0, w, w, ALU.is_ge, ALU.mult)],
                  reads=[r_qs[0]], writes=[r_qs[2]])
                G("vector", [lambda e: e.tensor_tensor(q0, q0, q2, ALU.subtract)],
                  reads=[r_qs[2]], writes=[r_qs[0]])
            G("vector", [lambda e: e.tensor_tensor(q1, q1, q0, ALU.subtract)],
              reads=[r_qs[0]], writes=[r_qs[1]])
            G("vector", [lambda e: e.tensor_tensor(q1, q1, q3, ALU.mult)],
              reads=[r_qs[3]], writes=[r_qs[1]])
            G("vector", [lambda e, x4t=x4t: e.tensor_scalar(
                q2, x4t, 0.0, 2.0, ALU.is_ge, ALU.mult)],
              reads=[r_sx4], writes=[r_qs[2]])
            G("vector", [lambda e: e.tensor_scalar(q2, q2, -1.0, None, ALU.add)],
              writes=[r_qs[2]])
            G("vector", [lambda e: e.tensor_tensor(htm[:], q1, q2, ALU.mult)],
              reads=[r_qs[1], r_qs[2]], writes=[r_htm])
            def tra2(e, tt=tt):
                src = htm[:, :].rearrange("p (c x) -> p c x", c=NC_D)
                dst = bass.AP(afm, tt * 128, [[NC_D * T, 128], [T, NC_D], [1, 128]])
                return e.dma_start(out=dst, in_=src, transpose=True)
            G("scalar", [tra2], reads=[r_htm], writes=[r_afm[tt]], sem="tda", dma=True)

        if DBG and s == 0:
            G("sync", [lambda e: e.dma_start(out=dbg["x4"], in_=sx4[:])], reads=[r_sx4], sem="od", dma=True)
            G("sync", [lambda e: e.dma_start(out=dbg["a0"], in_=htm[:])], reads=[r_htm], sem="od", dma=True)
        if s + 1 < NSLAB:
            phase_A(s + 1)
        for gi, (tt, nh) in enumerate(
                [(t_, n_) for t_ in range(NTT) for n_ in range(2)]):
            bank = gi % 4
            fns = []
            for k in range(NC_D):
                fns.append(lambda e, k=k, tt=tt, nh=nh, bank=bank: e.matmul(
                    ps[bank][:],
                    afm[:, k * T + tt * 128: k * T + (tt + 1) * 128],
                    ufm[:, k * 1024 + nh * 512: k * 1024 + (nh + 1) * 512],
                    start=(k == 0), stop=False))
            fns.append(lambda e, nh=nh, bank=bank: e.matmul(
                ps[bank][:], ones_c[:], bdw_r[:, nh * 512:(nh + 1) * 512],
                start=False, stop=True))
            G("tensor", fns, reads=[r_afm[tt], r_rows] + r_ufm[:2 * NC_D],
              writes=[r_ps[bank]])
            G("vector", [lambda e, tt=tt, nh=nh, bank=bank: e.scalar_tensor_tensor(
                x2b[:, tt * D + nh * 512: tt * D + (nh + 1) * 512],
                ps[bank][:], wscol[:],
                sx4[:, tt * D + nh * 512: tt * D + (nh + 1) * 512],
                ALU.mult, ALU.add)],
              reads=[r_ps[bank], r_sx4, r_rows], writes=[r_x2])

        ln_stats_slab(lambda tt: x2b[:, tt * D:(tt + 1) * D], [r_x2], 24, 28)
        for tt in range(NTT):
            z5t = x2b[:, tt * D:(tt + 1) * D]
            G("vector", [lambda e, z5t=z5t, tt=tt: e.tensor_scalar(
                qs[:, 3 * D:4 * D], z5t, sc[:, 24 + tt:25 + tt],
                sc[:, 28 + tt:29 + tt], ALU.subtract, ALU.mult)],
              reads=[r_x2, r_stats], writes=[r_qs[3]])
            def st_out(e, s=s, tt=tt):
                dst = y_h[(s * NTT + tt) * 128:(s * NTT + tt + 1) * 128, :]
                return e.dma_start(out=dst, in_=qs[:, 3 * D:4 * D])
            G("sync", [st_out], reads=[r_qs[3]], sem="od", dma=True)

    final_od = p.sem_v["od"]
    p._app("sync", lambda e, v=final_od: e.wait_ge(sems["od"], v))
    maxg = getattr(build, "max_groups", None)
    if maxg is not None:
        for e_ in ENGS:
            p.ops[e_] = [(g, f) for (g, f) in p.ops[e_] if g <= maxg]
        print(f"[build] truncated to groups <= {maxg}")

    with es:
        with nc.Block() as block:
            @block.sync
            def _(eng):
                for _g, f in p.ops["sync"]:
                    f(eng)

            @block.vector
            def _(eng):
                for _g, f in p.ops["vector"]:
                    f(eng)

            @block.scalar
            def _(eng):
                for _g, f in p.ops["scalar"]:
                    f(eng)

            @block.tensor
            def _(eng):
                for _g, f in p.ops["tensor"]:
                    f(eng)

            @block.gpsimd
            def _(eng):
                for _g, f in p.ops["gpsimd"]:
                    f(eng)
    return nc


def kernel(L0_attn_x, L0_attn_wq, L0_attn_wk, L0_attn_wv, L0_attn_wo,
           L0_attn_proj, L0_attn_w1, L0_attn_b1, L0_attn_w2, L0_attn_b2,
           L0_attn_g1, L0_attn_beta1, L0_attn_g2, L0_attn_beta2,
           L0_ffn_weight, L0_ffn_weight_scale, L0_ffn_bias):
    x = np.ascontiguousarray(np.asarray(L0_attn_x, dtype=np.float32))
    w8t_u8 = np.ascontiguousarray(np.asarray(L0_ffn_weight).T).view(np.uint8)
    ws_col = np.ascontiguousarray(np.broadcast_to(
        np.asarray(L0_ffn_weight_scale, dtype=np.float32).reshape(1, 1),
        (128, 1)))
    mask2 = np.ascontiguousarray(np.triu(np.ones((128, 128), np.float32)))
    row = lambda a: np.ascontiguousarray(np.asarray(a, np.float32).reshape(1, -1))

    common = {
        "wq": np.ascontiguousarray(np.asarray(L0_attn_wq, np.float32)),
        "wk": np.ascontiguousarray(np.asarray(L0_attn_wk, np.float32)),
        "wv": np.ascontiguousarray(np.asarray(L0_attn_wv, np.float32)),
        "wo": np.ascontiguousarray(np.asarray(L0_attn_wo, np.float32)),
        "proj": np.ascontiguousarray(np.asarray(L0_attn_proj, np.float32)),
        "w1": np.ascontiguousarray(np.asarray(L0_attn_w1, np.float32)),
        "b1": row(L0_attn_b1),
        "w2": np.ascontiguousarray(np.asarray(L0_attn_w2, np.float32)),
        "b2": row(L0_attn_b2),
        "g1": row(L0_attn_g1),
        "be1": row(L0_attn_beta1),
        "g2": row(L0_attn_g2),
        "be2": row(L0_attn_beta2),
        "w8t": w8t_u8,
        "fbias": row(L0_ffn_bias),
        "wsc": ws_col,
        "mask2": mask2,
    }
    in_maps = []
    per_core_b = B // N_CORES
    for c in range(N_CORES):
        mm = dict(common)
        mm["x"] = np.ascontiguousarray(
            x[c * per_core_b:(c + 1) * per_core_b].reshape(TOK, D))
        in_maps.append(mm)

    nc = bass.Bass("TRN2", target_bir_lowering=False, detect_race_conditions=False)
    build(nc)
    res = run_bass_kernel_spmd(nc, in_maps, core_ids=list(range(N_CORES)))
    out = np.empty((B, S, D), np.float32)
    for c in range(N_CORES):
        out[c * per_core_b:(c + 1) * per_core_b] = (
            res.results[c]["y"].reshape(per_core_b, S, D))
    return out


# version 10
# speedup vs baseline: 1.0081x; 1.0081x over previous
import math
import numpy as np
import concourse.bass as bass
import concourse.mybir as mybir
from concourse.bass_utils import run_bass_kernel_spmd

F32 = mybir.dt.float32
BF16 = mybir.dt.bfloat16
U8 = mybir.dt.uint8
ALU = mybir.AluOpType
AF = mybir.ActivationFunctionType

N_CORES = 8
B, S, D, H, DH, FF = 16, 2048, 1024, 16, 64, 4096
TOK = (B // N_CORES) * S
T = 512
NSLAB = TOK // T
NTT = T // 128
NC_D = D // 128
NC_F = FF // 128
NPAIR = H // 2
LN2 = math.log(2.0)
INV_LN2 = 1.0 / LN2

ENGS = ("sync", "vector", "scalar", "tensor", "gpsimd")
UNORDERED_SEMS = {"ld", "swq", "swk", "swv", "swo", "sw1", "sw2", "sw8", "w8l"}


class Res:
    __slots__ = ("w", "r")
    def __init__(self):
        self.w = None
        self.r = []


class Prog:
    def __init__(self):
        self.gidx = 0
        self.ops = {e: [] for e in ENGS}
        self.sem_h = {}
        self.sem_v = {}
        self.hw = {e: {} for e in ENGS}

    def sem(self, name, handle):
        self.sem_h[name] = handle
        self.sem_v[name] = 0

    def _app(self, eng, fn):
        self.ops[eng].append((self.gidx, fn))

    def _wait(self, eng, tok):
        if tok is None:
            return
        teng, sname, val, force = tok
        if sname in UNORDERED_SEMS:
            val = self.sem_v[sname]
        if teng == eng and not force:
            return
        if self.hw[eng].get(sname, 0) >= val:
            return
        self.hw[eng][sname] = val
        h = self.sem_h[sname]
        self._app(eng, lambda e, h=h, v=val: e.wait_ge(h, v))

    def group(self, eng, fns, reads=(), writes=(), sem=None, dma=False,
              force=False):
        self.gidx += 1
        for r in reads:
            self._wait(eng, r.w)
        for w in writes:
            self._wait(eng, w.w)
            for t in list(w.r):
                self._wait(eng, t)
        if sem is None:
            sem = {"sync": "sd", "vector": "vv", "scalar": "aa",
                   "tensor": "mm", "gpsimd": "gg"}[eng]
        if dma:
            force = True
        step = 16 if dma else 1
        self.sem_v[sem] += step
        val = self.sem_v[sem]
        h = self.sem_h[sem]
        for f in fns[:-1]:
            self._app(eng, f)
        last = fns[-1]
        self._app(eng, lambda e, f=last, h=h, s=step: f(e).then_inc(h, s))
        tok = (eng, sem, val, force)
        for r in reads:
            r.r.append(tok)
        for w in writes:
            w.w = tok
            w.r = []
        return tok


def build(nc):
    x_h = nc.dram_tensor("x", [TOK, D], F32, kind="ExternalInput").ap()
    y_h = nc.dram_tensor("y", [TOK, D], F32, kind="ExternalOutput").ap()
    wq_h = nc.dram_tensor("wq", [D, D], F32, kind="ExternalInput").ap()
    wk_h = nc.dram_tensor("wk", [D, D], F32, kind="ExternalInput").ap()
    wv_h = nc.dram_tensor("wv", [D, D], F32, kind="ExternalInput").ap()
    wo_h = nc.dram_tensor("wo", [D, D], F32, kind="ExternalInput").ap()
    proj_h = nc.dram_tensor("proj", [DH, DH], F32, kind="ExternalInput").ap()
    w1_h = nc.dram_tensor("w1", [D, FF], F32, kind="ExternalInput").ap()
    b1_h = nc.dram_tensor("b1", [1, FF], F32, kind="ExternalInput").ap()
    w2_h = nc.dram_tensor("w2", [FF, D], F32, kind="ExternalInput").ap()
    b2_h = nc.dram_tensor("b2", [1, D], F32, kind="ExternalInput").ap()
    g1_h = nc.dram_tensor("g1", [1, D], F32, kind="ExternalInput").ap()
    be1_h = nc.dram_tensor("be1", [1, D], F32, kind="ExternalInput").ap()
    g2_h = nc.dram_tensor("g2", [1, D], F32, kind="ExternalInput").ap()
    be2_h = nc.dram_tensor("be2", [1, D], F32, kind="ExternalInput").ap()
    w8_h = nc.dram_tensor("w8t", [D, D], U8, kind="ExternalInput").ap()
    fbias_h = nc.dram_tensor("fbias", [1, D], F32, kind="ExternalInput").ap()
    wsc_h = nc.dram_tensor("wsc", [128, 1], F32, kind="ExternalInput").ap()
    mask_h = nc.dram_tensor("mask2", [128, 128], F32, kind="ExternalInput").ap()
    wqs_h = nc.dram_tensor("wqs", [NC_D, NC_D, 128, 128], BF16).ap()
    wks_h = nc.dram_tensor("wks", [NC_D, NC_D, 128, 128], BF16).ap()
    wvs_h = nc.dram_tensor("wvs", [NC_D, 128, D], BF16).ap()
    wos_h = nc.dram_tensor("wos", [NC_D, 128, D], BF16).ap()
    w1s_h = nc.dram_tensor("w1s", [NC_F, NC_D, 128, 128], BF16).ap()
    w2s_h = nc.dram_tensor("w2s", [NC_F, 128, D], BF16).ap()
    w8s_h = nc.dram_tensor("w8s", [NC_D, 128, D], BF16).ap()
    DBG = getattr(build, "debug", False)
    if DBG:
        dbg = {n: nc.dram_tensor("dbg_" + n, [128, shp], dt, kind="ExternalOutput").ap()
               for n, shp, dt in (
                   ("htm0", D, BF16), ("qp", NC_D * T, BF16), ("kp", NC_D * T, BF16),
                   ("vpad", NTT * H * 65, BF16), ("atm0", D, BF16),
                   ("x2", NTT * D, F32), ("z", NTT * D, F32), ("x4", NTT * D, F32),
                   ("a0", D, BF16), ("hfm", NC_D * T, BF16), ("ktm", NC_D * T, BF16),
                   ("ufm", NC_F * T, BF16), ("kvb", NPAIR * 2 * 65, BF16),
                   ("qraw0", T, BF16), ("tq0", T, BF16), ("rq0", T, BF16),
                   ("st6", 12, F32), ("scc", 32, F32), ("htm1", D, BF16),
                   ("z5", NTT * D, F32), ("afm2", NC_D * T, BF16),
               )}

    import contextlib
    es = contextlib.ExitStack()
    sb = lambda name, shape, dt: es.enter_context(nc.sbuf_tensor(name, shape, dt))
    psb = lambda name, shape: es.enter_context(nc.psum_tensor(name, shape, F32))

    x_ab = sb("x_ab", [128, 2 * NTT * D], F32)
    x2b = sb("x2b", [128, NTT * D], F32)
    sx4 = sb("sx4", [128, NTT * D], F32)
    htm = sb("htm", [128, D], BF16)
    hfm = sb("hfm", [128, NC_D * T], BF16)
    qkraw = sb("qkraw", [128, 2 * T], BF16)
    qp = sb("qp", [128, NC_D * T], BF16)
    kp = sb("kp", [128, NC_D * T], BF16)
    tqr = sb("tqr", [128, 2 * T], BF16)
    vpad = sb("vpad", [128, NTT * H * 65], BF16)
    atm = sb("atm", [128, D], BF16)
    afm = sb("afm", [128, NC_D * T], BF16)
    ktmb = sb("ktmb", [128, NTT * D], BF16)
    kvs = sb("kvs", [128, NPAIR * 2 * 65], F32)
    kvb = sb("kvb", [128, NPAIR * 2 * 65], BF16)
    ufm = sb("ufm", [128, NC_F * T], BF16)
    qs = sb("qs", [128, 4 * D], F32)
    otmp = sb("otmp", [128, 4 * 65], F32)
    dsc = sb("dsc", [128, 32], F32)
    cb_w = sb("cb_w", [128, 3 * NC_D * 128], BF16)
    kc_w = sb("kc_w", [128, 4 * D], BF16)
    wq_w = wk_w = w1_w = cb_w
    wv_w = wo_w = w2_w = w8_w = kc_w
    fw = sb("fw", [128, 2 * 1024], F32)
    fu8 = sb("fu8", [128, 512], U8)
    g1c = sb("g1c", [128, NC_D], F32)
    be1c = sb("be1c", [128, NC_D], F32)
    g2c = sb("g2c", [128, NC_D], F32)
    be2c = sb("be2c", [128, NC_D], F32)
    bwqc = sb("bwqc", [128, NC_D], F32)
    bwkc = sb("bwkc", [128, NC_D], F32)
    gelb = sb("gelb", [128, NC_F], F32)
    bwv_r = sb("bwv_r", [1, D], BF16)
    b2_r = sb("b2_r", [1, D], BF16)
    bdw_r = sb("bdw_r", [1, D], BF16)
    ones_c = sb("ones_c", [1, 128], BF16)
    projb = sb("projb", [128, 64], BF16)
    maskb = sb("maskb", [128, 128], F32)
    wscol = sb("wscol", [128, 1], F32)
    rws = sb("rws", [128, 1], F32)
    epsc = sb("epsc", [128, 1], F32)
    tinyc = sb("tinyc", [128, 1], F32)
    st6 = sb("st6", [128, 48], F32)
    sc = sb("sc", [128, 96], F32)
    ps = [psb(f"ps{i}", [128, 512]) for i in range(8)]

    sems = {}
    for name in ("ld", "fwa", "fwb", "f8", "cb0", "cb1", "cb2", "cb3",
                 "kc0", "kc1", "kc2", "kc3",
                 "xd0", "xd1", "tdh", "tdq", "tda", "od",
                 "gb0", "gb1", "gb2",
                 "sd", "mm", "vv", "aa", "gg",
                 "swq", "swk", "swv", "swo", "sw1", "sw2", "sw8", "w8l"):
        sems[name] = es.enter_context(nc.semaphore(name))

    p = Prog()
    for n, h in sems.items():
        p.sem(n, h)

    r_ps = [Res() for _ in range(8)]
    r_x = [Res(), Res()]
    r_x2 = Res(); r_sx4 = Res()
    r_htm = Res(); r_hfm = [Res() for _ in range(NTT)]
    r_qkraw = [Res(), Res()]
    r_qp = [Res() for _ in range(NC_D)]
    r_kp = [Res() for _ in range(NC_D)]
    r_tqr = [Res(), Res()]
    r_vpad = [Res() for _ in range(NTT)]
    r_atm = Res(); r_afm = [Res() for _ in range(NTT)]
    r_ktm = [Res() for _ in range(NC_D)]
    r_kvs = [[Res(), Res()] for _ in range(NPAIR)]
    r_ufm = [Res() for _ in range(NC_F)]
    r_qs = [Res() for _ in range(4)]
    r_cb = [Res(), Res(), Res()]
    r_kc = [Res(), Res(), Res(), Res()]
    r_ww = {"wq": r_cb, "wk": r_cb, "w1": r_cb,
            "wv": r_kc, "wo": r_kc, "w2": r_kc, "w8": r_kc}
    r_fw = [Res(), Res()]
    r_fu8 = Res()
    r_sw = {n: Res() for n in ("wqs", "wks", "wvs", "wos", "w1s", "w2s", "w8s")}
    r_stats = Res()
    r_dsc = [Res() for _ in range(NPAIR)]
    r_otmp = [Res(), Res(), Res(), Res()]
    r_rows = Res()

    G = p.group

    def col_load(dst, src_ap, ncols):
        ap = bass.AP(src_ap.tensor, 0, [[1, 128], [128, ncols]])
        return lambda e, dst=dst, ap=ap, n=ncols: e.dma_start(
            out=dst[:, 0:n], in_=ap, allow_slow_non_contiguous=True)

    G("sync", [col_load(g1c, g1_h, NC_D)], writes=[r_rows], sem="ld", dma=True)
    G("sync", [col_load(be1c, be1_h, NC_D)], writes=[r_rows], sem="ld", dma=True)
    G("sync", [col_load(g2c, g2_h, NC_D)], writes=[r_rows], sem="ld", dma=True)
    G("sync", [col_load(be2c, be2_h, NC_D)], writes=[r_rows], sem="ld", dma=True)
    G("sync", [lambda e: e.dma_start(out=maskb[:], in_=mask_h)], writes=[r_rows], sem="ld", dma=True)
    G("sync", [lambda e: e.dma_start(out=wscol[:], in_=wsc_h)], writes=[r_rows], sem="ld", dma=True)
    rowf = fw[0:1, 1024:2048]
    G("sync", [lambda e: e.dma_start(out=rowf, in_=fbias_h)], writes=[r_fw[1]], sem="ld", dma=True)
    G("sync", [lambda e: e.dma_start(out=fw[0:64, 0:64], in_=proj_h)], writes=[r_fw[0]], sem="ld", dma=True)
    G("sync", [lambda e: e.dma_start(out=fw[64:128, 0:64], in_=proj_h)], writes=[r_fw[0]], sem="ld", dma=True)

    G("vector", [lambda e: e.memset(epsc[:], 1e-5)], writes=[r_rows])
    G("vector", [lambda e: e.memset(tinyc[:], 1e-35)], writes=[r_rows])
    G("vector", [lambda e: e.memset(ones_c[:], 1.0)], writes=[r_rows])
    for tt in range(NTT):
        oc = vpad[:, tt * H * 65:(tt + 1) * H * 65].rearrange(
            "p (h y) -> p h y", h=H)[:, :, 64:65]
        G("vector", [lambda e, o=oc: e.memset(o, 1.0)], writes=[r_vpad[tt]])
    G("vector", [lambda e: e.tensor_copy(projb[:], fw[:, 0:64])],
      reads=[r_fw[0]], writes=[r_rows], force=True)
    G("vector", [lambda e: e.reciprocal(rws[:], wscol[:])],
      reads=[r_rows], writes=[r_rows], force=True)
    G("vector", [lambda e: e.tensor_scalar(bdw_r[:], rowf, rws[0:1, :], None,
                                           ALU.mult)],
      reads=[r_fw[1], r_rows], writes=[r_rows], force=True)
    G("sync", [lambda e: e.dma_start(out=rowf, in_=b2_h)],
      writes=[r_fw[1]], sem="ld", dma=True)
    G("vector", [lambda e: e.tensor_copy(b2_r[:], rowf)],
      reads=[r_fw[1]], writes=[r_rows], force=True)

    def prep_qk(w_h, ws_h, bwc, rs_w, sw_sem):
        for k in range(NC_D):
            pk = k % 2
            G("sync", [lambda e, k=k, pk=pk: e.dma_start(
                out=fw[:, pk * 1024:(pk + 1) * 1024],
                in_=w_h[k * 128:(k + 1) * 128, :])],
              writes=[r_fw[pk]], sem=("fwa" if pk == 0 else "fwb"), dma=True)
            G("vector", [lambda e, k=k, pk=pk: e.tensor_scalar(
                htm[:], fw[:, pk * 1024:(pk + 1) * 1024], g1c[:, k:k + 1],
                None, ALU.mult)],
              reads=[r_fw[pk], r_rows], writes=[r_htm])
            fns = []
            for c in range(NC_D):
                fns.append(lambda e, pk=pk, c=c, k=k: e.matmul(
                    ps[6][:, c:c + 1],
                    fw[:, pk * 1024 + c * 128: pk * 1024 + (c + 1) * 128],
                    be1c[:, k:k + 1], start=True, stop=True))
            G("tensor", fns, reads=[r_fw[pk], r_rows], writes=[r_ps[6]])
            if k == 0:
                G("vector", [lambda e: e.tensor_copy(bwc[:, 0:NC_D], ps[6][:, 0:NC_D])],
                  reads=[r_ps[6]], writes=[r_rows], force=True)
            else:
                G("vector", [lambda e: e.tensor_tensor(
                    bwc[:, 0:NC_D], bwc[:, 0:NC_D], ps[6][:, 0:NC_D], ALU.add)],
                  reads=[r_ps[6]], writes=[r_rows], force=True)
            def store(e, k=k):
                src = htm[:, :].rearrange("p (c x) -> p c x", c=NC_D)
                dst = bass.AP(ws_h.tensor, k * (128 * 128 * NC_D) // NC_D * 1,
                              [[128, 128], [NC_D * 128 * 128, NC_D], [1, 128]])
                dst = bass.AP(ws_h.tensor, k * 128 * 128,
                              [[128, 128], [NC_D * 128 * 128, NC_D], [1, 128]])
                return e.dma_start(out=dst, in_=src)
            G("sync", [store], reads=[r_htm], writes=[rs_w], sem=sw_sem, dma=True)

    prep_qk(wq_h, wqs_h, bwqc, r_sw["wqs"], "swq")
    prep_qk(wk_h, wks_h, bwkc, r_sw["wks"], "swk")

    for k in range(NC_D):
        pk = k % 2
        G("sync", [lambda e, k=k, pk=pk: e.dma_start(
            out=fw[:, pk * 1024:(pk + 1) * 1024],
            in_=wv_h[k * 128:(k + 1) * 128, :])],
          writes=[r_fw[pk]], sem=("fwa" if pk == 0 else "fwb"), dma=True)
        G("vector", [lambda e, k=k, pk=pk: e.tensor_scalar(
            htm[:], fw[:, pk * 1024:(pk + 1) * 1024], g1c[:, k:k + 1],
            None, ALU.mult)],
          reads=[r_fw[pk], r_rows], writes=[r_htm])
        fns = []
        for half in range(2):
            fns.append(lambda e, k=k, pk=pk, half=half: e.matmul(
                ps[6 + half][0:1, 0:512],
                be1c[:, k:k + 1],
                fw[:, pk * 1024 + half * 512: pk * 1024 + (half + 1) * 512],
                start=(k == 0), stop=(k == NC_D - 1)))
        G("tensor", fns, reads=[r_fw[pk], r_rows], writes=[r_ps[6], r_ps[7]])
        G("sync", [lambda e, k=k: e.dma_start(out=wvs_h[k], in_=htm[:])],
          reads=[r_htm], writes=[r_sw["wvs"]], sem="swv", dma=True)
    G("vector", [lambda e: e.tensor_copy(bwv_r[:, 0:512], ps[6][0:1, 0:512])],
      reads=[r_ps[6]], writes=[r_rows], force=True)
    G("vector", [lambda e: e.tensor_copy(bwv_r[:, 512:1024], ps[7][0:1, 0:512])],
      reads=[r_ps[7]], writes=[r_rows], force=True)

    for k in range(NC_D):
        pk = k % 2
        G("sync", [lambda e, k=k, pk=pk: e.dma_start(
            out=fw[:, pk * 1024:(pk + 1) * 1024],
            in_=wo_h[k * 128:(k + 1) * 128, :])],
          writes=[r_fw[pk]], sem=("fwa" if pk == 0 else "fwb"), dma=True)
        G("vector", [lambda e, pk=pk: e.tensor_copy(
            htm[:], fw[:, pk * 1024:(pk + 1) * 1024])],
          reads=[r_fw[pk]], writes=[r_htm])
        G("sync", [lambda e, k=k: e.dma_start(out=wos_h[k], in_=htm[:])],
          reads=[r_htm], writes=[r_sw["wos"]], sem="swo", dma=True)

    G("sync", [col_load(gelb, b1_h, NC_F)], writes=[r_rows], sem="ld", dma=True)
    for k in range(NC_D):
        for q in range(4):
            pk = (k * 4 + q) % 2
            G("sync", [lambda e, k=k, q=q, pk=pk: e.dma_start(
                out=fw[:, pk * 1024:(pk + 1) * 1024],
                in_=w1_h[k * 128:(k + 1) * 128, q * 1024:(q + 1) * 1024])],
              writes=[r_fw[pk]], sem=("fwa" if pk == 0 else "fwb"), dma=True)
            G("vector", [lambda e, k=k, pk=pk: e.tensor_scalar(
                htm[:], fw[:, pk * 1024:(pk + 1) * 1024], g2c[:, k:k + 1],
                None, ALU.mult)],
              reads=[r_fw[pk], r_rows], writes=[r_htm])
            fns = []
            for c in range(8):
                fns.append(lambda e, pk=pk, c=c, k=k: e.matmul(
                    ps[7][:, c:c + 1],
                    fw[:, pk * 1024 + c * 128: pk * 1024 + (c + 1) * 128],
                    be2c[:, k:k + 1], start=True, stop=True))
            G("tensor", fns, reads=[r_fw[pk], r_rows], writes=[r_ps[7]])
            G("vector", [lambda e, q=q: e.tensor_tensor(
                gelb[:, q * 8:(q + 1) * 8], gelb[:, q * 8:(q + 1) * 8],
                ps[7][:, 0:8], ALU.add)],
              reads=[r_ps[7]], writes=[r_rows], force=True)
            def store1(e, k=k, q=q):
                src = htm[:, :].rearrange("p (c x) -> p c x", c=8)
                dst = bass.AP(w1s_h.tensor,
                              q * 8 * (NC_D * 128 * 128) + k * 128 * 128,
                              [[128, 128], [NC_D * 128 * 128, 8], [1, 128]])
                return e.dma_start(out=dst, in_=src)
            G("sync", [store1], reads=[r_htm], writes=[r_sw["w1s"]], sem="sw1", dma=True)
    for k in range(NC_F):
        pk = k % 2
        G("sync", [lambda e, k=k, pk=pk: e.dma_start(
            out=fw[:, pk * 1024:(pk + 1) * 1024],
            in_=w2_h[k * 128:(k + 1) * 128, :])],
          writes=[r_fw[pk]], sem=("fwa" if pk == 0 else "fwb"), dma=True)
        G("vector", [lambda e, pk=pk: e.tensor_copy(
            htm[:], fw[:, pk * 1024:(pk + 1) * 1024])],
          reads=[r_fw[pk]], writes=[r_htm])
        G("sync", [lambda e, k=k: e.dma_start(out=w2s_h[k], in_=htm[:])],
          reads=[r_htm], writes=[r_sw["w2s"]], sem="sw2", dma=True)

    for k in range(NC_D):
        u = fw[:, 0:1024]
        w0 = hfm[:, 0:1024]
        w1_ = hfm[:, 1024:2048]
        w2_ = hfm[:, 2048:3072]
        w3_ = hfm[:, 3072:4096]
        for hf in range(2):
            G("sync", [lambda e, k=k, hf=hf: e.dma_start(
                out=fu8[:], in_=w8_h[k * 128:(k + 1) * 128, hf * 512:(hf + 1) * 512])],
              writes=[r_fu8], sem="f8", dma=True)
            G("vector", [lambda e, hf=hf: e.tensor_copy(
                u[:, hf * 512:(hf + 1) * 512], fu8[:])],
              reads=[r_fu8], writes=[r_fw[0]])
        fns_v = [
            lambda e: e.tensor_scalar(w0, u, 128.0, 128.0, ALU.is_ge, ALU.mult),
            lambda e: e.tensor_tensor(w1_, u, w0, ALU.subtract),
            lambda e: e.tensor_scalar(w2_, w1_, 64.0, 64.0, ALU.is_ge, ALU.mult),
            lambda e: e.tensor_tensor(w1_, w1_, w2_, ALU.subtract),
            lambda e: e.tensor_scalar(w3_, w1_, 32.0, 32.0, ALU.is_ge, ALU.mult),
            lambda e: e.tensor_tensor(w1_, w1_, w3_, ALU.subtract),
            lambda e: e.tensor_tensor(w2_, w2_, w3_, ALU.add),
            lambda e: e.tensor_scalar(w3_, w1_, 16.0, 16.0, ALU.is_ge, ALU.mult),
            lambda e: e.tensor_tensor(w1_, w1_, w3_, ALU.subtract),
            lambda e: e.tensor_tensor(w2_, w2_, w3_, ALU.add),
            lambda e: e.tensor_scalar(w3_, w1_, 8.0, 8.0, ALU.is_ge, ALU.mult),
            lambda e: e.tensor_tensor(w1_, w1_, w3_, ALU.subtract),
            lambda e: e.tensor_tensor(w2_, w2_, w3_, ALU.add),
            lambda e: e.tensor_scalar(w3_, w2_, 8.0, 8.0, ALU.is_ge, ALU.mult),
            lambda e: e.tensor_tensor(w1_, w1_, w3_, ALU.add),
            lambda e: e.tensor_scalar(w2_, w2_, 8.0, -80.0, ALU.max, ALU.add),
        ]
        G("vector", fns_v, reads=[r_fu8], writes=[r_hfm[0], r_fw[0]])
        G("scalar", [lambda e, w2_=w2_: e.activation(
            w2_, w2_, AF.Exp, scale=LN2 / 8.0)],
          reads=[r_hfm[0]], writes=[r_hfm[0]])
        fns_v2 = [
            lambda e: e.tensor_tensor(w1_, w1_, w2_, ALU.mult),
            lambda e: e.tensor_scalar(w0, w0, -1.0 / 64.0, 1.0, ALU.mult, ALU.add),
            lambda e: e.tensor_tensor(w3_, w1_, w0, ALU.mult),
        ]
        G("vector", fns_v2, reads=[r_hfm[0]], writes=[r_hfm[0]])
        G("sync", [lambda e, k=k, w3_=w3_: e.dma_start(out=w8s_h[k], in_=w3_)],
          reads=[r_hfm[0]], writes=[r_sw["w8s"]], sem="sw8", dma=True)

    def load_cblocks(ws4_h, rs, c, slot_r, window, pk, q="sync"):
        def f(e, c=c, pk=pk):
            dst = window[:, pk * NC_D * 128:(pk + 1) * NC_D * 128].rearrange(
                "p (k x) -> p k x", k=NC_D)
            srcap = bass.AP(ws4_h.tensor, c * NC_D * 128 * 128,
                            [[128, 128], [128 * 128, NC_D], [1, 128]])
            return e.dma_start(out=dst, in_=srcap)
        sem = ("gb%d" if q == "gpsimd" else "cb%d") % pk
        G(q, [f], reads=[rs], writes=[slot_r], sem=sem, dma=True)

    def load_kchunk(ws_h, rs, k, slot_r, window, pk):
        def f(e, k=k, pk=pk):
            return e.dma_start(out=window[:, pk * D:(pk + 1) * D], in_=ws_h[k])
        G("sync", [f], reads=[rs], writes=[slot_r],
          sem="kc%d" % pk, dma=True)

    def load_x(s, buf):
        def f(e, s=s, buf=buf):
            src = bass.AP(x_h.tensor, s * T * D,
                          [[D, 128], [128 * D, NTT], [1, D]])
            return e.dma_start(out=x_ab[:, buf * NTT * D:(buf + 1) * NTT * D]
                               .rearrange("p (tt d) -> p tt d", tt=NTT),
                               in_=src)
        G("gpsimd", [f], writes=[r_x[buf]], sem=("xd0" if buf == 0 else "xd1"), dma=True)

    load_x(0, 0)

    r_sd8 = [Res() for _ in range(8)]

    def ln_stats_slab(tile_of, reads, i_m, i_a):
        for tt in range(NTT):
            ta = tile_of(tt)
            G("vector", [lambda e, ta=ta, tt=tt: e.tensor_scalar(
                qs[:, 0:D], ta, 0.0, None, ALU.add, ALU.add,
                accum_out=sc[:, 32 + tt:33 + tt])],
              reads=list(reads), writes=[r_sd8[2 * tt]], force=True)
            G("vector", [lambda e, ta=ta, tt=tt: e.scalar_tensor_tensor(
                qs[:, D:2 * D], ta, 1.0, ta, ALU.mult, ALU.mult,
                accum_out=sc[:, 36 + tt:37 + tt])],
              reads=list(reads), writes=[r_sd8[2 * tt + 1]], force=True)
        G("vector", [lambda e, i_m=i_m: e.tensor_scalar(
            sc[:, i_m:i_m + 4], sc[:, 32:36], 1.0 / D, None, ALU.mult)],
          reads=list(r_sd8), writes=[r_stats], force=True)
        G("vector", [lambda e, i_m=i_m: e.tensor_tensor(
            sc[:, 40:44], sc[:, i_m:i_m + 4], sc[:, i_m:i_m + 4], ALU.mult)],
          writes=[r_stats], force=True)
        G("vector", [lambda e: e.scalar_tensor_tensor(
            sc[:, 44:48], sc[:, 36:40], 1.0 / D, sc[:, 40:44],
            ALU.mult, ALU.subtract)],
          writes=[r_stats], force=True)
        G("scalar", [lambda e: e.activation(
            sc[:, 56:60], sc[:, 44:48], AF.Ln, bias=epsc[:])],
          reads=[r_rows], writes=[r_stats], force=True)
        G("scalar", [lambda e, i_a=i_a: e.activation(
            sc[:, i_a:i_a + 4], sc[:, 56:60], AF.Exp, scale=-0.5)],
          writes=[r_stats], force=True)

    def phase_A(s):
        xb = s % 2
        xoff = xb * NTT * D
        ln_stats_slab(lambda tt: x_ab[:, xoff + tt * D: xoff + (tt + 1) * D],
                      [r_x[xb]], 0, 4)
        if DBG and s == 0:
            G("sync", [lambda e: e.dma_start(out=dbg["st6"], in_=st6[:, 0:12])],
              reads=[r_stats], sem="od", dma=True)
            G("sync", [lambda e: e.dma_start(out=dbg["scc"][:, 0:8], in_=sc[:, 0:8])],
              reads=[r_stats], sem="od", dma=True)
        for tt in range(NTT):
            xt = x_ab[:, xoff + tt * D: xoff + (tt + 1) * D]
            G("vector", [lambda e, xt=xt, tt=tt: e.tensor_scalar(
                htm[:], xt, sc[:, 0 + tt:1 + tt], sc[:, 4 + tt:5 + tt],
                ALU.subtract, ALU.mult)],
              reads=[r_x[xb], r_stats], writes=[r_htm])
            if DBG and s == 0 and tt == 0:
                G("sync", [lambda e: e.dma_start(out=dbg["htm1"], in_=htm[:])],
                  reads=[r_htm], sem="od", dma=True)
            def tra(e, tt=tt):
                src = htm[:, :].rearrange("p (c x) -> p c x", c=NC_D)
                dst = bass.AP(hfm, tt * 128, [[NC_D * T, 128], [T, NC_D], [1, 128]])
                return e.dma_start(out=dst, in_=src, transpose=True)
            G("scalar", [tra], reads=[r_htm], writes=[r_hfm[tt]], sem="tdh", dma=True)

    phase_A(0)
    for s in range(NSLAB):
        xb = s % 2
        xoff = xb * NTT * D
        if s + 1 < NSLAB:
            load_x(s + 1, (s + 1) % 2)

        if DBG and s == 0:
            G("sync", [lambda e: e.dma_start(out=dbg["hfm"], in_=hfm[:])],
              reads=list(r_hfm), sem="od", dma=True)
        vgrp = [(tt, nh) for tt in range(NTT) for nh in range(2)]
        for k in range(NC_D):
            pk = k % 4
            load_kchunk(wvs_h, r_sw["wvs"], k, r_ww["wv"][pk], wv_w, pk)
            for gi, (tt, nh) in enumerate(vgrp):
                bank = gi
                fns = [lambda e, k=k, pk=pk, tt=tt, nh=nh, bank=bank: e.matmul(
                    ps[bank][:],
                    hfm[:, k * T + tt * 128: k * T + (tt + 1) * 128],
                    wv_w[:, pk * D + nh * 512: pk * D + (nh + 1) * 512],
                    start=(k == 0), stop=False)]
                if k == NC_D - 1:
                    fns.append(lambda e, nh=nh, bank=bank: e.matmul(
                        ps[bank][:], ones_c[:],
                        bwv_r[:, nh * 512:(nh + 1) * 512],
                        start=False, stop=True))
                G("tensor", fns, reads=[r_hfm[tt], r_ww["wv"][pk], r_rows],
                  writes=[r_ps[bank]])
        for gi, (tt, nh) in enumerate(vgrp):
            bank = gi
            def vevac(e, tt=tt, nh=nh, bank=bank):
                src = ps[bank][:, :].rearrange("p (h y) -> p h y", h=8)
                base = tt * H * 65 + nh * 8 * 65
                dst = vpad[:, base: base + 8 * 65].rearrange(
                    "p (h y) -> p h y", h=8)[:, :, 0:64]
                return e.tensor_copy(dst, src)
            G("vector", [vevac], reads=[r_ps[bank]], writes=[r_vpad[tt]])

        for wname, ws4, bwc, rdst, dsttile, window in (
                ("wq", wqs_h, bwqc, r_qp, qp, wq_w),
                ("wk", wks_h, bwkc, r_kp, kp, wk_w)):
            def stage1(c, wname=wname, ws4=ws4, window=window):
                pk = c % 3
                load_cblocks(ws4, r_sw["wqs" if wname == "wq" else "wks"], c,
                             r_ww[wname][pk], window, pk)
                bank = 4 + (c % 2)
                fns = []
                for k in range(NC_D):
                    fns.append(lambda e, k=k, pk=pk, bank=bank, window=window: e.matmul(
                        ps[bank][:],
                        window[:, (pk * NC_D + k) * 128:(pk * NC_D + k + 1) * 128],
                        hfm[:, k * T:(k + 1) * T],
                        start=(k == 0), stop=(k == NC_D - 1)))
                G("tensor", fns, reads=list(r_hfm) + [r_ww[wname][pk]], writes=[r_ps[bank]])
                slot = c % 2
                G("scalar", [lambda e, bank=bank, slot=slot, bwc=bwc, c=c:
                             e.activation(qkraw[:, slot * T:(slot + 1) * T],
                                          ps[bank][:], AF.Identity,
                                          bias=bwc[:, c:c + 1])],
                  reads=[r_ps[bank], r_rows], writes=[r_qkraw[slot]])
            def stage2(c, wname=wname, rdst=rdst, dsttile=dsttile):
                slot = c % 2
                fns2 = [
                    lambda e, slot=slot: e.matmul(
                        ps[6][0:64, :], projb[0:64, :],
                        qkraw[0:64, slot * T:(slot + 1) * T], start=True, stop=True),
                    lambda e, slot=slot: e.matmul(
                        ps[6][64:128, :], projb[64:128, :],
                        qkraw[64:128, slot * T:(slot + 1) * T], start=True, stop=True),
                ]
                G("tensor", fns2, reads=[r_qkraw[slot], r_rows], writes=[r_ps[6]])
                fns3 = [
                    lambda e: e.activation(tqr[:, 0:T], ps[6][:], AF.Exp),
                    lambda e: e.activation(tqr[:, T:2 * T], ps[6][:], AF.Relu),
                ]
                G("scalar", fns3, reads=[r_ps[6]], writes=[r_tqr[0], r_tqr[1]])
                G("vector", [lambda e, c=c, dsttile=dsttile: e.scalar_tensor_tensor(
                    dsttile[:, c * T:(c + 1) * T], tqr[:, 0:T], 1.0,
                    tqr[:, T:2 * T], ALU.min, ALU.add)],
                  reads=[r_tqr[0], r_tqr[1]], writes=[rdst[c]])
                if wname == "wk":
                    def trk(e, c=c):
                        srcT = kp[:, c * T:(c + 1) * T].rearrange("p (t x) -> p t x", t=NTT)
                        dstT = bass.AP(ktmb, c * 128, [[NTT * D, 128], [D, NTT], [1, 128]])
                        return e.dma_start(out=dstT, in_=srcT, transpose=True)
                    G("scalar", [trk], reads=[r_kp[c]], writes=[r_ktm[c]],
                      sem="tdh", dma=True)
            for c in range(NC_D):
                stage1(c)
                if c > 0:
                    stage2(c - 1)
            stage2(NC_D - 1)

        if DBG and s == 0:
            G("sync", [lambda e: e.dma_start(out=dbg["qp"], in_=qp[:])], reads=[r_qp[7]], sem="od", dma=True)
            G("sync", [lambda e: e.dma_start(out=dbg["kp"], in_=kp[:])], reads=[r_kp[7]], sem="od", dma=True)
            G("sync", [lambda e: e.dma_start(out=dbg["vpad"], in_=vpad[:])], reads=[r_vpad[3]], sem="od", dma=True)
            G("sync", [lambda e: e.dma_start(out=dbg["ktm"], in_=ktmb[:])], reads=list(r_ktm), sem="od", dma=True)
        ktm = ktmb
        for ch in range(NTT):
            seq_ch = (s * NTT + ch) % (4 * NTT)
            par_r = seq_ch % 2
            par_w = (seq_ch + 1) % 2

            def stage1(pr, ch=ch):
                c = pr
                qoff = c * T + ch * 128
                se = 0 if pr % 2 == 0 else 6
                so = 1 if pr % 2 == 0 else 7
                G("tensor", [lambda e, qoff=qoff, se=se: e.matmul(
                    ps[se][:, 0:128],
                    kp[0:64, qoff:qoff + 128], qp[0:64, qoff:qoff + 128],
                    start=True, stop=True)],
                  reads=[r_kp[c], r_qp[c]], writes=[r_ps[se]])
                G("tensor", [lambda e, qoff=qoff, so=so: e.matmul(
                    ps[so][:, 0:128],
                    kp[64:128, qoff:qoff + 128], qp[64:128, qoff:qoff + 128],
                    start=True, stop=True)],
                  reads=[r_kp[c], r_qp[c]], writes=[r_ps[so]])
                sb_off = (pr % 2) * 512
                ssb = tqr[:, sb_off:sb_off + 256]
                G("vector", [
                    lambda e, ssb=ssb, se=se: e.tensor_tensor(
                        ssb[:, 0:128], ps[se][:, 0:128], maskb[:, 0:128], ALU.mult),
                    lambda e, ssb=ssb, so=so: e.tensor_tensor(
                        ssb[:, 128:256], ps[so][:, 0:128], maskb[:, 0:128], ALU.mult)],
                  reads=[r_ps[se], r_ps[so], r_rows], writes=[r_tqr[pr % 2]])

            def stage2(pr, ch=ch, seq_ch=seq_ch, par_r=par_r, par_w=par_w):
                c = pr
                qoff = c * T + ch * 128
                sb_off = (pr % 2) * 512
                ssb = tqr[:, sb_off:sb_off + 256]
                kv_r = pr * 130 + par_r * 65
                kv_w = pr * 130 + par_w * 65
                fns = [lambda e, ssb=ssb, ch=ch, pr=pr, st=(seq_ch == 0): e.matmul(
                    ps[2][:, 0:65], ssb[:, 0:128],
                    vpad[:, ch * H * 65 + (pr * 2) * 65: ch * H * 65 + (pr * 2) * 65 + 65],
                    start=True, stop=st)]
                if seq_ch != 0:
                    fns.append(lambda e, qoff=qoff, kv_r=kv_r: e.matmul(
                        ps[2][:, 0:65], qp[0:64, qoff:qoff + 128],
                        kvb[0:64, kv_r:kv_r + 65], start=False, stop=True))
                rd = [r_tqr[pr % 2], r_vpad[ch], r_qp[c]]
                if seq_ch != 0:
                    rd.append(r_kvs[pr][par_r])
                G("tensor", fns, reads=rd, writes=[r_ps[2]])
                G("tensor", [lambda e, ssb=ssb, ch=ch, pr=pr: e.matmul(
                    ps[3][:, 0:65], ssb[:, 128:256],
                    vpad[:, ch * H * 65 + (pr * 2 + 1) * 65: ch * H * 65 + (pr * 2 + 1) * 65 + 65],
                    start=True, stop=True)],
                  reads=[r_tqr[pr % 2], r_vpad[ch]], writes=[r_ps[3]])
                if seq_ch != 0:
                    G("tensor", [lambda e, qoff=qoff, kv_r=kv_r: e.matmul(
                        ps[4][:, 0:65], qp[64:128, qoff:qoff + 128],
                        kvb[64:128, kv_r:kv_r + 65], start=True, stop=True)],
                      reads=[r_qp[c], r_kvs[pr][par_r]], writes=[r_ps[4]])
                eslot = otmp[:, (pr % 2) * 65:(pr % 2) * 65 + 65]
                oslot = otmp[:, (2 + pr % 2) * 65:(2 + pr % 2) * 65 + 65]
                G("vector", [lambda e, eslot=eslot: e.tensor_copy(eslot, ps[2][:, 0:65])],
                  reads=[r_ps[2]], writes=[r_otmp[pr % 2]], force=True)
                if seq_ch != 0:
                    G("vector", [lambda e, oslot=oslot: e.tensor_copy(oslot, ps[4][:, 0:65])],
                      reads=[r_ps[4]], writes=[r_otmp[2 + pr % 2]], force=True)
                    G("vector", [lambda e, oslot=oslot: e.tensor_tensor(
                        oslot, oslot, ps[3][:, 0:65], ALU.add)],
                      reads=[r_ps[3]], writes=[r_otmp[2 + pr % 2]], force=True)
                else:
                    G("vector", [lambda e, oslot=oslot: e.tensor_copy(oslot, ps[3][:, 0:65])],
                      reads=[r_ps[3]], writes=[r_otmp[2 + pr % 2]], force=True)
                dn = dsc[:, pr * 4:pr * 4 + 2]
                rdn = dsc[:, pr * 4 + 2:pr * 4 + 4]
                G("vector", [
                    lambda e, dn=dn, eslot=eslot: e.tensor_scalar(
                        dn[:, 0:1], eslot[:, 64:65], 1e-6, None, ALU.add),
                    lambda e, dn=dn, oslot=oslot: e.tensor_scalar(
                        dn[:, 1:2], oslot[:, 64:65], 1e-6, None, ALU.add)],
                  reads=[r_otmp[pr % 2], r_otmp[2 + pr % 2]],
                  writes=[r_dsc[pr]], force=True)
                G("vector", [lambda e, rdn=rdn, dn=dn: e.reciprocal(rdn, dn)],
                  writes=[r_dsc[pr]], force=True)
                head_e = pr * 2
                G("scalar", [
                    lambda e, head_e=head_e, rdn=rdn, eslot=eslot: e.activation(
                        atm[:, head_e * 64:(head_e + 1) * 64],
                        eslot[:, 0:64], AF.Identity, scale=rdn[:, 0:1]),
                    lambda e, head_e=head_e, rdn=rdn, oslot=oslot: e.activation(
                        atm[:, (head_e + 1) * 64:(head_e + 2) * 64],
                        oslot[:, 0:64], AF.Identity, scale=rdn[:, 1:2])],
                  reads=[r_otmp[pr % 2], r_otmp[2 + pr % 2], r_dsc[pr]],
                  writes=[r_atm])
                fns = []
                for hh in range(2):
                    koff = ch * D + c * 128 + hh * 64
                    vb = ch * H * 65 + (pr * 2 + hh) * 65
                    fns.append(lambda e, koff=koff, vb=vb, hh=hh: e.matmul(
                        ps[5][hh * 64:(hh + 1) * 64, 0:65],
                        ktm[:, koff:koff + 64],
                        vpad[:, vb:vb + 65], start=True, stop=True))
                G("tensor", fns, reads=[r_ktm[c], r_vpad[ch]], writes=[r_ps[5]])
                dst = kvs[:, kv_w:kv_w + 65]
                dstb = kvb[:, kv_w:kv_w + 65]
                srcp = ps[5][:, 0:65]
                old_ = kvs[:, kv_r:kv_r + 65]
                rd = [r_ps[5]] + ([r_kvs[pr][par_r]] if seq_ch != 0 else [])
                if seq_ch == 0:
                    G("vector", [lambda e, dst=dst, srcp=srcp: e.tensor_copy(dst, srcp)],
                      reads=rd, writes=[r_kvs[pr][par_w]], force=True)
                else:
                    G("vector", [lambda e, dst=dst, srcp=srcp, old_=old_:
                                 e.tensor_tensor(dst, old_, srcp, ALU.add)],
                      reads=rd, writes=[r_kvs[pr][par_w]], force=True)
                G("vector", [lambda e, dst=dst, dstb=dstb: e.tensor_copy(dstb, dst)],
                  reads=[r_kvs[pr][par_w]], writes=[r_kvs[pr][par_w]], force=True)

            for pr in range(NPAIR):
                stage1(pr)
                if pr > 0:
                    stage2(pr - 1)
            stage2(NPAIR - 1)
            def trat(e, ch=ch):
                src = atm[:, :].rearrange("p (c x) -> p c x", c=NC_D)
                dst = bass.AP(afm, ch * 128, [[NC_D * T, 128], [T, NC_D], [1, 128]])
                return e.dma_start(out=dst, in_=src, transpose=True)
            G("scalar", [trat], reads=[r_atm], writes=[r_afm[ch]], sem="tda", dma=True)

        dgrp = [(tt, nh) for tt in range(NTT) for nh in range(2)]
        for k in range(NC_D):
            pk = k % 4
            load_kchunk(wos_h, r_sw["wos"], k, r_ww["wo"][pk], wo_w, pk)
            for gi, (tt, nh) in enumerate(dgrp):
                bank = gi
                G("tensor", [lambda e, k=k, pk=pk, tt=tt, nh=nh, bank=bank: e.matmul(
                    ps[bank][:],
                    afm[:, k * T + tt * 128: k * T + (tt + 1) * 128],
                    wo_w[:, pk * D + nh * 512: pk * D + (nh + 1) * 512],
                    start=(k == 0), stop=(k == NC_D - 1))],
                  reads=[r_afm[tt], r_ww["wo"][pk]], writes=[r_ps[bank]])
        for gi, (tt, nh) in enumerate(dgrp):
            bank = gi
            G("vector", [lambda e, tt=tt, nh=nh, bank=bank, xoff=xoff:
                         e.scalar_tensor_tensor(
                x2b[:, tt * D + nh * 512: tt * D + (nh + 1) * 512],
                ps[bank][:], 1.0,
                x_ab[:, xoff + tt * D + nh * 512: xoff + tt * D + (nh + 1) * 512],
                ALU.mult, ALU.add)],
              reads=[r_ps[bank], r_x[xb]], writes=[r_x2])

        if DBG and s == 0:
            G("sync", [lambda e: e.dma_start(out=dbg["x2"], in_=x2b[:])], reads=[r_x2], sem="od", dma=True)
        ln_stats_slab(lambda tt: x2b[:, tt * D:(tt + 1) * D], [r_x2], 8, 12)
        for tt in range(NTT):
            x2t = x2b[:, tt * D:(tt + 1) * D]
            G("vector", [lambda e, x2t=x2t, tt=tt: e.tensor_scalar(
                htm[:], x2t, sc[:, 8 + tt:9 + tt], sc[:, 12 + tt:13 + tt],
                ALU.subtract, ALU.mult)],
              reads=[r_x2, r_stats], writes=[r_htm])
            def trh2(e, tt=tt):
                src = htm[:, :].rearrange("p (c x) -> p c x", c=NC_D)
                dst = bass.AP(kp, tt * 128, [[NC_D * T, 128], [T, NC_D], [1, 128]])
                return e.dma_start(out=dst, in_=src, transpose=True)
            G("scalar", [trh2], reads=[r_htm], writes=list(r_kp), sem="tdq", dma=True)
            G("gpsimd", [lambda e, tt=tt, xoff=xoff: e.tensor_tensor(
                sx4[:, tt * D:(tt + 1) * D],
                x_ab[:, xoff + tt * D: xoff + (tt + 1) * D],
                x2b[:, tt * D:(tt + 1) * D], ALU.add)],
              reads=[r_x[xb], r_x2], writes=[r_sx4])

        h2fm = kp
        for fc in range(NC_F):
            pk = fc % 3
            load_cblocks(w1s_h, r_sw["w1s"], fc, r_ww["w1"][pk], w1_w, pk, q="gpsimd")
            bank = 4 + (fc % 2)
            fns = []
            for k in range(NC_D):
                fns.append(lambda e, k=k, pk=pk, bank=bank: e.matmul(
                    ps[bank][:],
                    w1_w[:, (pk * NC_D + k) * 128:(pk * NC_D + k + 1) * 128],
                    h2fm[:, k * T:(k + 1) * T],
                    start=(k == 0), stop=(k == NC_D - 1)))
            G("tensor", fns, reads=list(r_kp) + [r_ww["w1"][pk]], writes=[r_ps[bank]])
            G("scalar", [lambda e, fc=fc, bank=bank: e.activation(
                ufm[:, fc * T:(fc + 1) * T], ps[bank][:], AF.Gelu,
                bias=gelb[:, fc:fc + 1])],
              reads=[r_ps[bank], r_rows], writes=[r_ufm[fc]])
        for fc in range(NC_F):
            pk = fc % 4
            load_kchunk(w2s_h, r_sw["w2s"], fc, r_ww["w2"][pk], w2_w, pk)
            for gi, (tt, nh) in enumerate([(t_, n_) for t_ in range(NTT) for n_ in range(2)]):
                bank = gi
                fns = [lambda e, fc=fc, pk=pk, tt=tt, nh=nh, bank=bank: e.matmul(
                    ps[bank][:],
                    ufm[:, fc * T + tt * 128: fc * T + (tt + 1) * 128],
                    w2_w[:, pk * D + nh * 512: pk * D + (nh + 1) * 512],
                    start=(fc == 0), stop=False)]
                if fc == NC_F - 1:
                    fns.append(lambda e, nh=nh, bank=bank: e.matmul(
                        ps[bank][:], ones_c[:],
                        b2_r[:, nh * 512:(nh + 1) * 512],
                        start=False, stop=True))
                G("tensor", fns,
                  reads=[r_ufm[fc], r_ww["w2"][pk], r_rows],
                  writes=[r_ps[bank]])
        for gi, (tt, nh) in enumerate([(t_, n_) for t_ in range(NTT) for n_ in range(2)]):
            bank = gi
            G("vector", [lambda e, tt=tt, nh=nh, bank=bank: e.scalar_tensor_tensor(
                x2b[:, tt * D + nh * 512: tt * D + (nh + 1) * 512],
                ps[bank][:], 1.0,
                sx4[:, tt * D + nh * 512: tt * D + (nh + 1) * 512],
                ALU.mult, ALU.add)],
              reads=[r_ps[bank], r_sx4], writes=[r_x2])

        for k in range(NC_D):
            def f8l(e, k=k):
                return e.dma_start(out=ufm[:, k * 1024:(k + 1) * 1024],
                                   in_=w8s_h[k])
            G("sync", [f8l], reads=[r_sw["w8s"]],
              writes=[r_ufm[2 * k], r_ufm[2 * k + 1]], sem="w8l", dma=True)

        ln_stats_slab(lambda tt: x2b[:, tt * D:(tt + 1) * D], [r_x2], 16, 20)
        for tt in range(NTT):
            zt = x2b[:, tt * D:(tt + 1) * D]
            x4t = sx4[:, tt * D:(tt + 1) * D]
            G("vector", [lambda e, zt=zt, x4t=x4t, tt=tt: e.tensor_scalar(
                x4t, zt, sc[:, 16 + tt:17 + tt], sc[:, 20 + tt:21 + tt],
                ALU.subtract, ALU.mult)],
              reads=[r_x2, r_stats], writes=[r_sx4])

        def quant_steps(tt, A, rA, B, rB, C, rC, c0, obuf, r_ob):
            x4t = sx4[:, tt * D:(tt + 1) * D]
            lf = x2b[:, tt * D:(tt + 1) * D]
            am = sc[:, c0:c0 + 1]; xs = sc[:, c0 + 1:c0 + 2]
            c6 = sc[:, c0 + 2:c0 + 3]; c9 = sc[:, c0 + 3:c0 + 4]
            yield ("vector", [lambda e: e.scalar_tensor_tensor(
                A, x4t, -1.0, x4t, ALU.mult, ALU.max)],
                [r_sx4], [rA], False)
            yield ("vector", [lambda e: e.tensor_scalar(
                B, A, 1e-30, None, ALU.max, ALU.max, accum_out=am)],
                [rA], [rB, r_stats], True)
            yield ("vector", [lambda e: e.tensor_scalar(
                xs, am, 1e-12, 1.0 / 448.0, ALU.max, ALU.mult)],
                [], [r_stats], True)
            yield ("scalar", [
                lambda e: e.activation(c6, xs, AF.Ln, bias=tinyc[:],
                                       scale=2.0 ** -6),
                lambda e: e.activation(c9, xs, AF.Ln, bias=tinyc[:],
                                       scale=2.0 ** -9),
                lambda e: e.activation(lf, B, AF.Ln, bias=tinyc[:])],
                [rB, r_rows, r_x2], [r_stats, r_x2], True)
            yield ("vector", [lambda e: e.tensor_scalar(
                B, lf, c6, INV_LN2, ALU.subtract, ALU.mult)],
                [r_x2, r_stats], [rB], False)
            for w in (8.0, 4.0, 2.0, 1.0):
                yield ("vector", [lambda e, w=w: e.tensor_scalar(
                    A, B, w, w, ALU.is_ge, ALU.mult)], [rB], [rA], False)
                yield ("vector", [lambda e: e.tensor_tensor(
                    B, B, A, ALU.subtract)], [rA], [rB], False)
            yield ("vector", [lambda e: e.tensor_scalar(
                C, lf, c6, INV_LN2, ALU.subtract, ALU.mult)],
                [r_x2, r_stats], [rC], False)
            yield ("vector", [lambda e: e.tensor_tensor(
                C, C, B, ALU.subtract)], [rB], [rC], False)
            yield ("scalar", [
                lambda e: e.activation(A, B, AF.Exp, scale=LN2),
                lambda e: e.activation(C, C, AF.Exp, bias=c9, scale=LN2)],
                [rB, rC, r_stats], [rA, rC], False)
            yield ("vector", [lambda e: e.tensor_scalar(
                B, A, 8.0, 0.5, ALU.mult, ALU.add)], [rA], [rB], False)
            yield ("vector", [lambda e: e.tensor_copy(A, B)],
                [rB], [rA], False)
            for w in (16.0, 8.0, 4.0, 2.0, 1.0):
                yield ("vector", [lambda e, w=w: e.tensor_scalar(
                    lf, A, w, w, ALU.is_ge, ALU.mult)],
                    [rA, r_x2], [r_x2], False)
                yield ("vector", [lambda e: e.tensor_tensor(
                    A, A, lf, ALU.subtract)], [r_x2], [rA], False)
            yield ("vector", [lambda e: e.tensor_tensor(
                B, B, A, ALU.subtract)], [rA], [rB], False)
            yield ("vector", [lambda e: e.tensor_tensor(
                B, B, C, ALU.mult)], [rC], [rB], False)
            yield ("vector", [lambda e: e.tensor_scalar(
                A, x4t, 0.0, 2.0, ALU.is_ge, ALU.mult)],
                [r_sx4], [rA], False)
            yield ("vector", [lambda e: e.tensor_scalar(
                A, A, -1.0, None, ALU.add)], [], [rA], False)
            yield ("vector", [lambda e: e.tensor_tensor(
                obuf[:], B, A, ALU.mult)], [rB, rA], [r_ob], False)

        for ttA in (0, 2):
            ttB = ttA + 1
            gA = quant_steps(ttA, qs[:, 0:D], r_qs[0], qs[:, D:2 * D],
                             r_qs[1], qs[:, 2 * D:3 * D], r_qs[2],
                             68, htm, r_htm)
            gB = quant_steps(ttB, qs[:, 3 * D:4 * D], r_qs[3],
                             fw[:, 0:1024], r_fw[0], fw[:, 1024:2048],
                             r_fw[1], 88, atm, r_atm)
            for stA, stB in zip(gA, gB):
                for (eng, fns, rds, wrs, fc) in (stA, stB):
                    G(eng, fns, reads=rds, writes=wrs, force=fc)
            for tt, obuf, r_ob in ((ttA, htm, r_htm), (ttB, atm, r_atm)):
                def tra2(e, tt=tt, obuf=obuf):
                    src = obuf[:, :].rearrange("p (c x) -> p c x", c=NC_D)
                    dst = bass.AP(afm, tt * 128,
                                  [[NC_D * T, 128], [T, NC_D], [1, 128]])
                    return e.dma_start(out=dst, in_=src, transpose=True)
                G("scalar", [tra2], reads=[r_ob], writes=[r_afm[tt]],
                  sem="tda", dma=True)

        if DBG and s == 0:
            G("sync", [lambda e: e.dma_start(out=dbg["x4"], in_=sx4[:])], reads=[r_sx4], sem="od", dma=True)
            G("sync", [lambda e: e.dma_start(out=dbg["a0"], in_=htm[:])], reads=[r_htm], sem="od", dma=True)
        if s + 1 < NSLAB:
            phase_A(s + 1)
        for gi, (tt, nh) in enumerate(
                [(t_, n_) for t_ in range(NTT) for n_ in range(2)]):
            bank = gi % 4
            fns = []
            for k in range(NC_D):
                fns.append(lambda e, k=k, tt=tt, nh=nh, bank=bank: e.matmul(
                    ps[bank][:],
                    afm[:, k * T + tt * 128: k * T + (tt + 1) * 128],
                    ufm[:, k * 1024 + nh * 512: k * 1024 + (nh + 1) * 512],
                    start=(k == 0), stop=False))
            fns.append(lambda e, nh=nh, bank=bank: e.matmul(
                ps[bank][:], ones_c[:], bdw_r[:, nh * 512:(nh + 1) * 512],
                start=False, stop=True))
            G("tensor", fns, reads=[r_afm[tt], r_rows] + r_ufm[:2 * NC_D],
              writes=[r_ps[bank]])
            G("vector", [lambda e, tt=tt, nh=nh, bank=bank: e.scalar_tensor_tensor(
                x2b[:, tt * D + nh * 512: tt * D + (nh + 1) * 512],
                ps[bank][:], wscol[:],
                sx4[:, tt * D + nh * 512: tt * D + (nh + 1) * 512],
                ALU.mult, ALU.add)],
              reads=[r_ps[bank], r_sx4, r_rows], writes=[r_x2])

        ln_stats_slab(lambda tt: x2b[:, tt * D:(tt + 1) * D], [r_x2], 24, 28)
        for tt in range(NTT):
            z5t = x2b[:, tt * D:(tt + 1) * D]
            G("vector", [lambda e, z5t=z5t, tt=tt: e.tensor_scalar(
                qs[:, 3 * D:4 * D], z5t, sc[:, 24 + tt:25 + tt],
                sc[:, 28 + tt:29 + tt], ALU.subtract, ALU.mult)],
              reads=[r_x2, r_stats], writes=[r_qs[3]])
            def st_out(e, s=s, tt=tt):
                dst = y_h[(s * NTT + tt) * 128:(s * NTT + tt + 1) * 128, :]
                return e.dma_start(out=dst, in_=qs[:, 3 * D:4 * D])
            G("sync", [st_out], reads=[r_qs[3]], sem="od", dma=True)

    final_od = p.sem_v["od"]
    p._app("sync", lambda e, v=final_od: e.wait_ge(sems["od"], v))
    maxg = getattr(build, "max_groups", None)
    if maxg is not None:
        for e_ in ENGS:
            p.ops[e_] = [(g, f) for (g, f) in p.ops[e_] if g <= maxg]
        print(f"[build] truncated to groups <= {maxg}")

    with es:
        with nc.Block() as block:
            @block.sync
            def _(eng):
                for _g, f in p.ops["sync"]:
                    f(eng)

            @block.vector
            def _(eng):
                for _g, f in p.ops["vector"]:
                    f(eng)

            @block.scalar
            def _(eng):
                for _g, f in p.ops["scalar"]:
                    f(eng)

            @block.tensor
            def _(eng):
                for _g, f in p.ops["tensor"]:
                    f(eng)

            @block.gpsimd
            def _(eng):
                for _g, f in p.ops["gpsimd"]:
                    f(eng)
    return nc


def kernel(L0_attn_x, L0_attn_wq, L0_attn_wk, L0_attn_wv, L0_attn_wo,
           L0_attn_proj, L0_attn_w1, L0_attn_b1, L0_attn_w2, L0_attn_b2,
           L0_attn_g1, L0_attn_beta1, L0_attn_g2, L0_attn_beta2,
           L0_ffn_weight, L0_ffn_weight_scale, L0_ffn_bias):
    x = np.ascontiguousarray(np.asarray(L0_attn_x, dtype=np.float32))
    w8t_u8 = np.ascontiguousarray(np.asarray(L0_ffn_weight).T).view(np.uint8)
    ws_col = np.ascontiguousarray(np.broadcast_to(
        np.asarray(L0_ffn_weight_scale, dtype=np.float32).reshape(1, 1),
        (128, 1)))
    mask2 = np.ascontiguousarray(np.triu(np.ones((128, 128), np.float32)))
    row = lambda a: np.ascontiguousarray(np.asarray(a, np.float32).reshape(1, -1))

    common = {
        "wq": np.ascontiguousarray(np.asarray(L0_attn_wq, np.float32)),
        "wk": np.ascontiguousarray(np.asarray(L0_attn_wk, np.float32)),
        "wv": np.ascontiguousarray(np.asarray(L0_attn_wv, np.float32)),
        "wo": np.ascontiguousarray(np.asarray(L0_attn_wo, np.float32)),
        "proj": np.ascontiguousarray(np.asarray(L0_attn_proj, np.float32)),
        "w1": np.ascontiguousarray(np.asarray(L0_attn_w1, np.float32)),
        "b1": row(L0_attn_b1),
        "w2": np.ascontiguousarray(np.asarray(L0_attn_w2, np.float32)),
        "b2": row(L0_attn_b2),
        "g1": row(L0_attn_g1),
        "be1": row(L0_attn_beta1),
        "g2": row(L0_attn_g2),
        "be2": row(L0_attn_beta2),
        "w8t": w8t_u8,
        "fbias": row(L0_ffn_bias),
        "wsc": ws_col,
        "mask2": mask2,
    }
    in_maps = []
    per_core_b = B // N_CORES
    for c in range(N_CORES):
        mm = dict(common)
        mm["x"] = np.ascontiguousarray(
            x[c * per_core_b:(c + 1) * per_core_b].reshape(TOK, D))
        in_maps.append(mm)

    nc = bass.Bass("TRN2", target_bir_lowering=False, detect_race_conditions=False)
    build(nc)
    res = run_bass_kernel_spmd(nc, in_maps, core_ids=list(range(N_CORES)))
    out = np.empty((B, S, D), np.float32)
    for c in range(N_CORES):
        out[c * per_core_b:(c + 1) * per_core_b] = (
            res.results[c]["y"].reshape(per_core_b, S, D))
    return out


# version 11
# speedup vs baseline: 1.0305x; 1.0221x over previous
import math
import numpy as np
import concourse.bass as bass
import concourse.mybir as mybir
from concourse.bass_utils import run_bass_kernel_spmd

F32 = mybir.dt.float32
BF16 = mybir.dt.bfloat16
U8 = mybir.dt.uint8
ALU = mybir.AluOpType
AF = mybir.ActivationFunctionType

N_CORES = 8
B, S, D, H, DH, FF = 16, 2048, 1024, 16, 64, 4096
TOK = (B // N_CORES) * S
T = 512
NSLAB = TOK // T
NTT = T // 128
NC_D = D // 128
NC_F = FF // 128
NPAIR = H // 2
LN2 = math.log(2.0)
INV_LN2 = 1.0 / LN2

ENGS = ("sync", "vector", "scalar", "tensor", "gpsimd")
UNORDERED_SEMS = {"ld", "swq", "swk", "swv", "swo", "sw1", "sw2", "sw8", "w8l"}


class Res:
    __slots__ = ("w", "r")
    def __init__(self):
        self.w = None
        self.r = []


class Prog:
    def __init__(self):
        self.gidx = 0
        self.ops = {e: [] for e in ENGS}
        self.sem_h = {}
        self.sem_v = {}
        self.hw = {e: {} for e in ENGS}

    def sem(self, name, handle):
        self.sem_h[name] = handle
        self.sem_v[name] = 0

    def _app(self, eng, fn):
        self.ops[eng].append((self.gidx, fn))

    def _wait(self, eng, tok):
        if tok is None:
            return
        teng, sname, val, force = tok
        if sname in UNORDERED_SEMS:
            val = self.sem_v[sname]
        if teng == eng and not force:
            return
        if self.hw[eng].get(sname, 0) >= val:
            return
        self.hw[eng][sname] = val
        h = self.sem_h[sname]
        self._app(eng, lambda e, h=h, v=val: e.wait_ge(h, v))

    def group(self, eng, fns, reads=(), writes=(), sem=None, dma=False,
              force=False):
        self.gidx += 1
        for r in reads:
            self._wait(eng, r.w)
        for w in writes:
            self._wait(eng, w.w)
            for t in list(w.r):
                self._wait(eng, t)
        if sem is None:
            sem = {"sync": "sd", "vector": "vv", "scalar": "aa",
                   "tensor": "mm", "gpsimd": "gg"}[eng]
        if dma:
            force = True
        step = 16 if dma else 1
        self.sem_v[sem] += step
        val = self.sem_v[sem]
        h = self.sem_h[sem]
        for f in fns[:-1]:
            self._app(eng, f)
        last = fns[-1]
        self._app(eng, lambda e, f=last, h=h, s=step: f(e).then_inc(h, s))
        tok = (eng, sem, val, force)
        for r in reads:
            r.r.append(tok)
        for w in writes:
            w.w = tok
            w.r = []
        return tok


def build(nc):
    x_h = nc.dram_tensor("x", [TOK, D], F32, kind="ExternalInput").ap()
    y_h = nc.dram_tensor("y", [TOK, D], F32, kind="ExternalOutput").ap()
    wq_h = nc.dram_tensor("wq", [D, D], F32, kind="ExternalInput").ap()
    wk_h = nc.dram_tensor("wk", [D, D], F32, kind="ExternalInput").ap()
    wv_h = nc.dram_tensor("wv", [D, D], F32, kind="ExternalInput").ap()
    wo_h = nc.dram_tensor("wo", [D, D], F32, kind="ExternalInput").ap()
    proj_h = nc.dram_tensor("proj", [DH, DH], F32, kind="ExternalInput").ap()
    w1_h = nc.dram_tensor("w1", [D, FF], F32, kind="ExternalInput").ap()
    b1_h = nc.dram_tensor("b1", [1, FF], F32, kind="ExternalInput").ap()
    w2_h = nc.dram_tensor("w2", [FF, D], F32, kind="ExternalInput").ap()
    b2_h = nc.dram_tensor("b2", [1, D], F32, kind="ExternalInput").ap()
    g1_h = nc.dram_tensor("g1", [1, D], F32, kind="ExternalInput").ap()
    be1_h = nc.dram_tensor("be1", [1, D], F32, kind="ExternalInput").ap()
    g2_h = nc.dram_tensor("g2", [1, D], F32, kind="ExternalInput").ap()
    be2_h = nc.dram_tensor("be2", [1, D], F32, kind="ExternalInput").ap()
    w8_h = nc.dram_tensor("w8t", [D, D], U8, kind="ExternalInput").ap()
    fbias_h = nc.dram_tensor("fbias", [1, D], F32, kind="ExternalInput").ap()
    wsc_h = nc.dram_tensor("wsc", [128, 1], F32, kind="ExternalInput").ap()
    mask_h = nc.dram_tensor("mask2", [128, 128], F32, kind="ExternalInput").ap()
    wqs_h = nc.dram_tensor("wqs", [NC_D, NC_D, 128, 128], BF16).ap()
    wks_h = nc.dram_tensor("wks", [NC_D, NC_D, 128, 128], BF16).ap()
    wvs_h = nc.dram_tensor("wvs", [NC_D, 128, D], BF16).ap()
    wos_h = nc.dram_tensor("wos", [NC_D, 128, D], BF16).ap()
    w1s_h = nc.dram_tensor("w1s", [NC_F, NC_D, 128, 128], BF16).ap()
    w2s_h = nc.dram_tensor("w2s", [NC_F, 128, D], BF16).ap()
    w8s_h = nc.dram_tensor("w8s", [NC_D, 128, D], BF16).ap()
    DBG = getattr(build, "debug", False)
    if DBG:
        dbg = {n: nc.dram_tensor("dbg_" + n, [128, shp], dt, kind="ExternalOutput").ap()
               for n, shp, dt in (
                   ("htm0", D, BF16), ("qp", NC_D * T, BF16), ("kp", NC_D * T, BF16),
                   ("vpad", NTT * H * 65, BF16), ("atm0", D, BF16),
                   ("x2", NTT * D, F32), ("z", NTT * D, F32), ("x4", NTT * D, F32),
                   ("a0", D, BF16), ("hfm", NC_D * T, BF16), ("ktm", NC_D * T, BF16),
                   ("ufm", NC_F * T, BF16), ("kvb", NPAIR * 2 * 65, BF16),
                   ("qraw0", T, BF16), ("tq0", T, BF16), ("rq0", T, BF16),
                   ("st6", 12, F32), ("scc", 32, F32), ("htm1", D, BF16),
                   ("z5", NTT * D, F32), ("afm2", NC_D * T, BF16),
               )}

    import contextlib
    es = contextlib.ExitStack()
    sb = lambda name, shape, dt: es.enter_context(nc.sbuf_tensor(name, shape, dt))
    psb = lambda name, shape: es.enter_context(nc.psum_tensor(name, shape, F32))

    x_ab = sb("x_ab", [128, 2 * NTT * D], F32)
    x2b = sb("x2b", [128, NTT * D], F32)
    sx4 = sb("sx4", [128, NTT * D], F32)
    htm = sb("htm", [128, D], BF16)
    hfm = sb("hfm", [128, NC_D * T], BF16)
    qkraw = sb("qkraw", [128, 2 * T], BF16)
    qp = sb("qp", [128, NC_D * T], BF16)
    kp = sb("kp", [128, NC_D * T], BF16)
    tqr = sb("tqr", [128, 2 * T], BF16)
    vpad = sb("vpad", [128, NTT * H * 65], BF16)
    atm = sb("atm", [128, D], BF16)
    afm = sb("afm", [128, NC_D * T], BF16)
    ktmb = sb("ktmb", [128, NTT * D], BF16)
    kvs = sb("kvs", [128, NPAIR * 2 * 65], F32)
    kvb = sb("kvb", [128, NPAIR * 2 * 65], BF16)
    ufm = sb("ufm", [128, NC_F * T], BF16)
    qs = sb("qs", [128, 4 * D], F32)
    otmp = sb("otmp", [128, 4 * 65], F32)
    dsc = sb("dsc", [128, 32], F32)
    cb_w = sb("cb_w", [128, 3 * NC_D * 128], BF16)
    kc_w = sb("kc_w", [128, 4 * D], BF16)
    wq_w = wk_w = w1_w = cb_w
    wv_w = wo_w = w2_w = w8_w = kc_w
    fw = sb("fw", [128, 2 * 1024], F32)
    fu8 = sb("fu8", [128, 512], U8)
    g1c = sb("g1c", [128, NC_D], F32)
    be1c = sb("be1c", [128, NC_D], F32)
    g2c = sb("g2c", [128, NC_D], F32)
    be2c = sb("be2c", [128, NC_D], F32)
    bwqc = sb("bwqc", [128, NC_D], F32)
    bwkc = sb("bwkc", [128, NC_D], F32)
    gelb = sb("gelb", [128, NC_F], F32)
    bwv_r = sb("bwv_r", [1, D], BF16)
    b2_r = sb("b2_r", [1, D], BF16)
    bdw_r = sb("bdw_r", [1, D], BF16)
    ones_c = sb("ones_c", [1, 128], BF16)
    projb = sb("projb", [128, 64], BF16)
    maskb = sb("maskb", [128, 128], F32)
    wscol = sb("wscol", [128, 1], F32)
    rws = sb("rws", [128, 1], F32)
    epsc = sb("epsc", [128, 1], F32)
    tinyc = sb("tinyc", [128, 1], F32)
    zeroc = sb("zeroc", [128, 1], F32)
    st6 = sb("st6", [128, 48], F32)
    sc = sb("sc", [128, 96], F32)
    ps = [psb(f"ps{i}", [128, 512]) for i in range(8)]

    sems = {}
    for name in ("ld", "fwa", "fwb", "f8", "cb0", "cb1", "cb2", "cb3",
                 "kc0", "kc1", "kc2", "kc3",
                 "xd0", "xd1", "tdh", "tdq", "tda", "od",
                 "gb0", "gb1", "gb2",
                 "sd", "mm", "vv", "aa", "gg",
                 "swq", "swk", "swv", "swo", "sw1", "sw2", "sw8", "w8l"):
        sems[name] = es.enter_context(nc.semaphore(name))

    p = Prog()
    for n, h in sems.items():
        p.sem(n, h)

    r_ps = [Res() for _ in range(8)]
    r_x = [Res(), Res()]
    r_x2 = Res(); r_sx4 = Res()
    r_htm = Res(); r_hfm = [Res() for _ in range(NTT)]
    r_qkraw = [Res(), Res()]
    r_qp = [Res() for _ in range(NC_D)]
    r_kp = [Res() for _ in range(NC_D)]
    r_tqr = [Res(), Res()]
    r_vpad = [Res() for _ in range(NTT)]
    r_atm = Res(); r_afm = [Res() for _ in range(NTT)]
    r_ktm = [Res() for _ in range(NC_D)]
    r_kvs = [[Res(), Res()] for _ in range(NPAIR)]
    r_ufm = [Res() for _ in range(NC_F)]
    r_qs = [Res() for _ in range(4)]
    r_cb = [Res(), Res(), Res()]
    r_kc = [Res(), Res(), Res(), Res()]
    r_ww = {"wq": r_cb, "wk": r_cb, "w1": r_cb,
            "wv": r_kc, "wo": r_kc, "w2": r_kc, "w8": r_kc}
    r_fw = [Res(), Res()]
    r_fu8 = Res()
    r_sw = {n: Res() for n in ("wqs", "wks", "wvs", "wos", "w1s", "w2s", "w8s")}
    r_stats = Res()
    r_dsc = [Res() for _ in range(NPAIR)]
    r_otmp = [Res(), Res(), Res(), Res()]
    r_rows = Res()

    G = p.group

    def col_load(dst, src_ap, ncols):
        ap = bass.AP(src_ap.tensor, 0, [[1, 128], [128, ncols]])
        return lambda e, dst=dst, ap=ap, n=ncols: e.dma_start(
            out=dst[:, 0:n], in_=ap, allow_slow_non_contiguous=True)

    G("sync", [col_load(g1c, g1_h, NC_D)], writes=[r_rows], sem="ld", dma=True)
    G("sync", [col_load(be1c, be1_h, NC_D)], writes=[r_rows], sem="ld", dma=True)
    G("sync", [col_load(g2c, g2_h, NC_D)], writes=[r_rows], sem="ld", dma=True)
    G("sync", [col_load(be2c, be2_h, NC_D)], writes=[r_rows], sem="ld", dma=True)
    G("sync", [lambda e: e.dma_start(out=maskb[:], in_=mask_h)], writes=[r_rows], sem="ld", dma=True)
    G("sync", [lambda e: e.dma_start(out=wscol[:], in_=wsc_h)], writes=[r_rows], sem="ld", dma=True)
    rowf = fw[0:1, 1024:2048]
    G("sync", [lambda e: e.dma_start(out=rowf, in_=fbias_h)], writes=[r_fw[1]], sem="ld", dma=True)
    G("sync", [lambda e: e.dma_start(out=fw[0:64, 0:64], in_=proj_h)], writes=[r_fw[0]], sem="ld", dma=True)
    G("sync", [lambda e: e.dma_start(out=fw[64:128, 0:64], in_=proj_h)], writes=[r_fw[0]], sem="ld", dma=True)

    G("vector", [lambda e: e.memset(epsc[:], 1e-5)], writes=[r_rows])
    G("vector", [lambda e: e.memset(zeroc[:], 0.0)], writes=[r_rows])
    G("vector", [lambda e: e.memset(tinyc[:], 1e-35)], writes=[r_rows])
    G("vector", [lambda e: e.memset(ones_c[:], 1.0)], writes=[r_rows])
    for tt in range(NTT):
        oc = vpad[:, tt * H * 65:(tt + 1) * H * 65].rearrange(
            "p (h y) -> p h y", h=H)[:, :, 64:65]
        G("vector", [lambda e, o=oc: e.memset(o, 1.0)], writes=[r_vpad[tt]])
    G("vector", [lambda e: e.tensor_copy(projb[:], fw[:, 0:64])],
      reads=[r_fw[0]], writes=[r_rows], force=True)
    G("vector", [lambda e: e.reciprocal(rws[:], wscol[:])],
      reads=[r_rows], writes=[r_rows], force=True)
    G("vector", [lambda e: e.tensor_scalar(bdw_r[:], rowf, rws[0:1, :], None,
                                           ALU.mult)],
      reads=[r_fw[1], r_rows], writes=[r_rows], force=True)
    G("sync", [lambda e: e.dma_start(out=rowf, in_=b2_h)],
      writes=[r_fw[1]], sem="ld", dma=True)
    G("vector", [lambda e: e.tensor_copy(b2_r[:], rowf)],
      reads=[r_fw[1]], writes=[r_rows], force=True)

    def prep_qk(w_h, ws_h, bwc, rs_w, sw_sem):
        for k in range(NC_D):
            pk = k % 2
            G("sync", [lambda e, k=k, pk=pk: e.dma_start(
                out=fw[:, pk * 1024:(pk + 1) * 1024],
                in_=w_h[k * 128:(k + 1) * 128, :])],
              writes=[r_fw[pk]], sem=("fwa" if pk == 0 else "fwb"), dma=True)
            stb = htm if pk == 0 else atm
            r_stb = r_htm if pk == 0 else r_atm
            G("vector", [lambda e, k=k, pk=pk, stb=stb: e.tensor_scalar(
                stb[:], fw[:, pk * 1024:(pk + 1) * 1024], g1c[:, k:k + 1],
                None, ALU.mult)],
              reads=[r_fw[pk], r_rows], writes=[r_stb])
            fns = []
            for c in range(NC_D):
                fns.append(lambda e, pk=pk, c=c, k=k: e.matmul(
                    ps[6][:, c:c + 1],
                    fw[:, pk * 1024 + c * 128: pk * 1024 + (c + 1) * 128],
                    be1c[:, k:k + 1], start=True, stop=True))
            G("tensor", fns, reads=[r_fw[pk], r_rows], writes=[r_ps[6]])
            if k == 0:
                G("vector", [lambda e: e.tensor_copy(bwc[:, 0:NC_D], ps[6][:, 0:NC_D])],
                  reads=[r_ps[6]], writes=[r_rows], force=True)
            else:
                G("vector", [lambda e: e.tensor_tensor(
                    bwc[:, 0:NC_D], bwc[:, 0:NC_D], ps[6][:, 0:NC_D], ALU.add)],
                  reads=[r_ps[6]], writes=[r_rows], force=True)
            def store(e, k=k, stb=stb):
                src = stb[:, :].rearrange("p (c x) -> p c x", c=NC_D)
                dst = bass.AP(ws_h.tensor, k * 128 * 128,
                              [[128, 128], [NC_D * 128 * 128, NC_D], [1, 128]])
                return e.dma_start(out=dst, in_=src)
            G("sync", [store], reads=[r_stb], writes=[rs_w], sem=sw_sem, dma=True)

    prep_qk(wq_h, wqs_h, bwqc, r_sw["wqs"], "swq")
    prep_qk(wk_h, wks_h, bwkc, r_sw["wks"], "swk")

    for k in range(NC_D):
        pk = k % 2
        G("sync", [lambda e, k=k, pk=pk: e.dma_start(
            out=fw[:, pk * 1024:(pk + 1) * 1024],
            in_=wv_h[k * 128:(k + 1) * 128, :])],
          writes=[r_fw[pk]], sem=("fwa" if pk == 0 else "fwb"), dma=True)
        stb = htm if pk == 0 else atm
        r_stb = r_htm if pk == 0 else r_atm
        G("vector", [lambda e, k=k, pk=pk, stb=stb: e.tensor_scalar(
            stb[:], fw[:, pk * 1024:(pk + 1) * 1024], g1c[:, k:k + 1],
            None, ALU.mult)],
          reads=[r_fw[pk], r_rows], writes=[r_stb])
        fns = []
        for half in range(2):
            fns.append(lambda e, k=k, pk=pk, half=half: e.matmul(
                ps[6 + half][0:1, 0:512],
                be1c[:, k:k + 1],
                fw[:, pk * 1024 + half * 512: pk * 1024 + (half + 1) * 512],
                start=(k == 0), stop=(k == NC_D - 1)))
        G("tensor", fns, reads=[r_fw[pk], r_rows], writes=[r_ps[6], r_ps[7]])
        G("sync", [lambda e, k=k, stb=stb: e.dma_start(out=wvs_h[k], in_=stb[:])],
          reads=[r_stb], writes=[r_sw["wvs"]], sem="swv", dma=True)
    G("vector", [lambda e: e.tensor_copy(bwv_r[:, 0:512], ps[6][0:1, 0:512])],
      reads=[r_ps[6]], writes=[r_rows], force=True)
    G("vector", [lambda e: e.tensor_copy(bwv_r[:, 512:1024], ps[7][0:1, 0:512])],
      reads=[r_ps[7]], writes=[r_rows], force=True)

    for k in range(NC_D):
        pk = k % 2
        G("sync", [lambda e, k=k, pk=pk: e.dma_start(
            out=fw[:, pk * 1024:(pk + 1) * 1024],
            in_=wo_h[k * 128:(k + 1) * 128, :])],
          writes=[r_fw[pk]], sem=("fwa" if pk == 0 else "fwb"), dma=True)
        stb = htm if pk == 0 else atm
        r_stb = r_htm if pk == 0 else r_atm
        G("scalar", [lambda e, pk=pk, stb=stb: e.activation(
            stb[:], fw[:, pk * 1024:(pk + 1) * 1024], AF.Identity,
            bias=zeroc[:])],
          reads=[r_fw[pk], r_rows], writes=[r_stb])
        G("sync", [lambda e, k=k, stb=stb: e.dma_start(out=wos_h[k], in_=stb[:])],
          reads=[r_stb], writes=[r_sw["wos"]], sem="swo", dma=True)

    G("sync", [col_load(gelb, b1_h, NC_F)], writes=[r_rows], sem="ld", dma=True)
    for k in range(NC_D):
        for q in range(4):
            pk = (k * 4 + q) % 2
            G("sync", [lambda e, k=k, q=q, pk=pk: e.dma_start(
                out=fw[:, pk * 1024:(pk + 1) * 1024],
                in_=w1_h[k * 128:(k + 1) * 128, q * 1024:(q + 1) * 1024])],
              writes=[r_fw[pk]], sem=("fwa" if pk == 0 else "fwb"), dma=True)
            stb = htm if pk == 0 else atm
            r_stb = r_htm if pk == 0 else r_atm
            G("vector", [lambda e, k=k, pk=pk, stb=stb: e.tensor_scalar(
                stb[:], fw[:, pk * 1024:(pk + 1) * 1024], g2c[:, k:k + 1],
                None, ALU.mult)],
              reads=[r_fw[pk], r_rows], writes=[r_stb])
            fns = []
            for c in range(8):
                fns.append(lambda e, pk=pk, c=c, k=k: e.matmul(
                    ps[7][:, c:c + 1],
                    fw[:, pk * 1024 + c * 128: pk * 1024 + (c + 1) * 128],
                    be2c[:, k:k + 1], start=True, stop=True))
            G("tensor", fns, reads=[r_fw[pk], r_rows], writes=[r_ps[7]])
            G("vector", [lambda e, q=q: e.tensor_tensor(
                gelb[:, q * 8:(q + 1) * 8], gelb[:, q * 8:(q + 1) * 8],
                ps[7][:, 0:8], ALU.add)],
              reads=[r_ps[7]], writes=[r_rows], force=True)
            def store1(e, k=k, q=q, stb=stb):
                src = stb[:, :].rearrange("p (c x) -> p c x", c=8)
                dst = bass.AP(w1s_h.tensor,
                              q * 8 * (NC_D * 128 * 128) + k * 128 * 128,
                              [[128, 128], [NC_D * 128 * 128, 8], [1, 128]])
                return e.dma_start(out=dst, in_=src)
            G("sync", [store1], reads=[r_stb], writes=[r_sw["w1s"]], sem="sw1", dma=True)
    def load_cblocks(ws4_h, rs, c, slot_r, window, pk, q="sync"):
        def f(e, c=c, pk=pk):
            dst = window[:, pk * NC_D * 128:(pk + 1) * NC_D * 128].rearrange(
                "p (k x) -> p k x", k=NC_D)
            srcap = bass.AP(ws4_h.tensor, c * NC_D * 128 * 128,
                            [[128, 128], [128 * 128, NC_D], [1, 128]])
            return e.dma_start(out=dst, in_=srcap)
        sem = ("gb%d" if q == "gpsimd" else "cb%d") % pk
        G(q, [f], reads=[rs], writes=[slot_r], sem=sem, dma=True)

    def load_kchunk(ws_h, rs, k, slot_r, window, pk):
        def f(e, k=k, pk=pk):
            return e.dma_start(out=window[:, pk * D:(pk + 1) * D], in_=ws_h[k])
        G("sync", [f], reads=[rs], writes=[slot_r],
          sem="kc%d" % pk, dma=True)

    def load_x(s, buf):
        def f(e, s=s, buf=buf):
            src = bass.AP(x_h.tensor, s * T * D,
                          [[D, 128], [128 * D, NTT], [1, D]])
            return e.dma_start(out=x_ab[:, buf * NTT * D:(buf + 1) * NTT * D]
                               .rearrange("p (tt d) -> p tt d", tt=NTT),
                               in_=src)
        G("gpsimd", [f], writes=[r_x[buf]], sem=("xd0" if buf == 0 else "xd1"), dma=True)

    load_x(0, 0)

    r_sd8 = [Res() for _ in range(8)]

    def ln_stats_slab(tile_of, reads, i_m, i_a):
        for tt in range(NTT):
            ta = tile_of(tt)
            G("vector", [lambda e, ta=ta, tt=tt: e.tensor_scalar(
                qs[:, 0:D], ta, 0.0, None, ALU.add, ALU.add,
                accum_out=sc[:, 32 + tt:33 + tt])],
              reads=list(reads), writes=[r_sd8[2 * tt]], force=True)
            G("vector", [lambda e, ta=ta, tt=tt: e.scalar_tensor_tensor(
                qs[:, D:2 * D], ta, 1.0, ta, ALU.mult, ALU.mult,
                accum_out=sc[:, 36 + tt:37 + tt])],
              reads=list(reads), writes=[r_sd8[2 * tt + 1]], force=True)
        G("vector", [lambda e, i_m=i_m: e.tensor_scalar(
            sc[:, i_m:i_m + 4], sc[:, 32:36], 1.0 / D, None, ALU.mult)],
          reads=list(r_sd8), writes=[r_stats], force=True)
        G("vector", [lambda e, i_m=i_m: e.tensor_tensor(
            sc[:, 40:44], sc[:, i_m:i_m + 4], sc[:, i_m:i_m + 4], ALU.mult)],
          writes=[r_stats], force=True)
        G("vector", [lambda e: e.scalar_tensor_tensor(
            sc[:, 44:48], sc[:, 36:40], 1.0 / D, sc[:, 40:44],
            ALU.mult, ALU.subtract)],
          writes=[r_stats], force=True)
        G("scalar", [lambda e: e.activation(
            sc[:, 56:60], sc[:, 44:48], AF.Ln, bias=epsc[:])],
          reads=[r_rows], writes=[r_stats], force=True)
        G("scalar", [lambda e, i_a=i_a: e.activation(
            sc[:, i_a:i_a + 4], sc[:, 56:60], AF.Exp, scale=-0.5)],
          writes=[r_stats], force=True)

    def phase_A(s):
        xb = s % 2
        xoff = xb * NTT * D
        ln_stats_slab(lambda tt: x_ab[:, xoff + tt * D: xoff + (tt + 1) * D],
                      [r_x[xb]], 0, 4)
        if DBG and s == 0:
            G("sync", [lambda e: e.dma_start(out=dbg["st6"], in_=st6[:, 0:12])],
              reads=[r_stats], sem="od", dma=True)
            G("sync", [lambda e: e.dma_start(out=dbg["scc"][:, 0:8], in_=sc[:, 0:8])],
              reads=[r_stats], sem="od", dma=True)
        for tt in range(NTT):
            xt = x_ab[:, xoff + tt * D: xoff + (tt + 1) * D]
            G("vector", [lambda e, xt=xt, tt=tt: e.tensor_scalar(
                htm[:], xt, sc[:, 0 + tt:1 + tt], sc[:, 4 + tt:5 + tt],
                ALU.subtract, ALU.mult)],
              reads=[r_x[xb], r_stats], writes=[r_htm])
            if DBG and s == 0 and tt == 0:
                G("sync", [lambda e: e.dma_start(out=dbg["htm1"], in_=htm[:])],
                  reads=[r_htm], sem="od", dma=True)
            def tra(e, tt=tt):
                src = htm[:, :].rearrange("p (c x) -> p c x", c=NC_D)
                dst = bass.AP(hfm, tt * 128, [[NC_D * T, 128], [T, NC_D], [1, 128]])
                return e.dma_start(out=dst, in_=src, transpose=True)
            G("scalar", [tra], reads=[r_htm], writes=[r_hfm[tt]], sem="tdh", dma=True)

    def phase_V():
        vgrp = [(tt, nh) for tt in range(NTT) for nh in range(2)]
        for k in range(NC_D):
            pk = k % 4
            load_kchunk(wvs_h, r_sw["wvs"], k, r_ww["wv"][pk], wv_w, pk)
            for gi, (tt, nh) in enumerate(vgrp):
                bank = gi
                fns = [lambda e, k=k, pk=pk, tt=tt, nh=nh, bank=bank: e.matmul(
                    ps[bank][:],
                    hfm[:, k * T + tt * 128: k * T + (tt + 1) * 128],
                    wv_w[:, pk * D + nh * 512: pk * D + (nh + 1) * 512],
                    start=(k == 0), stop=False)]
                if k == NC_D - 1:
                    fns.append(lambda e, nh=nh, bank=bank: e.matmul(
                        ps[bank][:], ones_c[:],
                        bwv_r[:, nh * 512:(nh + 1) * 512],
                        start=False, stop=True))
                G("tensor", fns, reads=[r_hfm[tt], r_ww["wv"][pk], r_rows],
                  writes=[r_ps[bank]])
        for gi, (tt, nh) in enumerate(vgrp):
            bank = gi
            def vevac(e, tt=tt, nh=nh, bank=bank):
                src = ps[bank][:, :].rearrange("p (h y) -> p h y", h=8)
                base = tt * H * 65 + nh * 8 * 65
                dst = vpad[:, base: base + 8 * 65].rearrange(
                    "p (h y) -> p h y", h=8)[:, :, 0:64]
                return e.tensor_copy(dst, src)
            G("vector", [vevac], reads=[r_ps[bank]], writes=[r_vpad[tt]])

    phase_A(0)
    phase_V()
    for k in range(NC_F):
        pk = k % 2
        G("sync", [lambda e, k=k, pk=pk: e.dma_start(
            out=fw[:, pk * 1024:(pk + 1) * 1024],
            in_=w2_h[k * 128:(k + 1) * 128, :])],
          writes=[r_fw[pk]], sem=("fwa" if pk == 0 else "fwb"), dma=True)
        stb = htm if pk == 0 else atm
        r_stb = r_htm if pk == 0 else r_atm
        G("scalar", [lambda e, pk=pk, stb=stb: e.activation(
            stb[:], fw[:, pk * 1024:(pk + 1) * 1024], AF.Identity,
            bias=zeroc[:])],
          reads=[r_fw[pk], r_rows], writes=[r_stb])
        G("sync", [lambda e, k=k, stb=stb: e.dma_start(out=w2s_h[k], in_=stb[:])],
          reads=[r_stb], writes=[r_sw["w2s"]], sem="sw2", dma=True)

    r_u8s = r_ufm[:8]
    for k in range(NC_D):
        u = fw[:, 0:1024]
        w0 = ufm[:, 0:1024]
        w1_ = ufm[:, 1024:2048]
        w2_ = ufm[:, 2048:3072]
        w3_ = ufm[:, 3072:4096]
        for hf in range(2):
            G("sync", [lambda e, k=k, hf=hf: e.dma_start(
                out=fu8[:], in_=w8_h[k * 128:(k + 1) * 128, hf * 512:(hf + 1) * 512])],
              writes=[r_fu8], sem="f8", dma=True)
            G("vector", [lambda e, hf=hf: e.tensor_copy(
                u[:, hf * 512:(hf + 1) * 512], fu8[:])],
              reads=[r_fu8], writes=[r_fw[0]])
        fns_v = [
            lambda e: e.tensor_scalar(w0, u, 128.0, 128.0, ALU.is_ge, ALU.mult),
            lambda e: e.tensor_tensor(w1_, u, w0, ALU.subtract),
            lambda e: e.tensor_scalar(w2_, w1_, 64.0, 64.0, ALU.is_ge, ALU.mult),
            lambda e: e.tensor_tensor(w1_, w1_, w2_, ALU.subtract),
            lambda e: e.tensor_scalar(w3_, w1_, 32.0, 32.0, ALU.is_ge, ALU.mult),
            lambda e: e.tensor_tensor(w1_, w1_, w3_, ALU.subtract),
            lambda e: e.tensor_tensor(w2_, w2_, w3_, ALU.add),
            lambda e: e.tensor_scalar(w3_, w1_, 16.0, 16.0, ALU.is_ge, ALU.mult),
            lambda e: e.tensor_tensor(w1_, w1_, w3_, ALU.subtract),
            lambda e: e.tensor_tensor(w2_, w2_, w3_, ALU.add),
            lambda e: e.tensor_scalar(w3_, w1_, 8.0, 8.0, ALU.is_ge, ALU.mult),
            lambda e: e.tensor_tensor(w1_, w1_, w3_, ALU.subtract),
            lambda e: e.tensor_tensor(w2_, w2_, w3_, ALU.add),
            lambda e: e.tensor_scalar(w3_, w2_, 8.0, 8.0, ALU.is_ge, ALU.mult),
            lambda e: e.tensor_tensor(w1_, w1_, w3_, ALU.add),
            lambda e: e.tensor_scalar(w2_, w2_, 8.0, -80.0, ALU.max, ALU.add),
        ]
        G("vector", fns_v, reads=[r_fu8], writes=r_u8s + [r_fw[0]])
        G("scalar", [lambda e, w2_=w2_: e.activation(
            w2_, w2_, AF.Exp, scale=LN2 / 8.0)],
          reads=[r_u8s[0]], writes=r_u8s)
        fns_v2 = [
            lambda e: e.tensor_tensor(w1_, w1_, w2_, ALU.mult),
            lambda e: e.tensor_scalar(w0, w0, -1.0 / 64.0, 1.0, ALU.mult, ALU.add),
            lambda e: e.tensor_tensor(w3_, w1_, w0, ALU.mult),
        ]
        G("vector", fns_v2, reads=[r_u8s[0]], writes=r_u8s)
        G("sync", [lambda e, k=k, w3_=w3_: e.dma_start(out=w8s_h[k], in_=w3_)],
          reads=list(r_u8s), writes=[r_sw["w8s"]], sem="sw8", dma=True)

    for s in range(NSLAB):
        xb = s % 2
        xoff = xb * NTT * D
        if s + 1 < NSLAB:
            load_x(s + 1, (s + 1) % 2)

        if DBG and s == 0:
            G("sync", [lambda e: e.dma_start(out=dbg["hfm"], in_=hfm[:])],
              reads=list(r_hfm), sem="od", dma=True)
        for wname, ws4, bwc, rdst, dsttile, window in (
                ("wq", wqs_h, bwqc, r_qp, qp, wq_w),
                ("wk", wks_h, bwkc, r_kp, kp, wk_w)):
            def stage1(c, wname=wname, ws4=ws4, window=window):
                pk = c % 3
                load_cblocks(ws4, r_sw["wqs" if wname == "wq" else "wks"], c,
                             r_ww[wname][pk], window, pk)
                bank = 4 + (c % 2)
                fns = []
                for k in range(NC_D):
                    fns.append(lambda e, k=k, pk=pk, bank=bank, window=window: e.matmul(
                        ps[bank][:],
                        window[:, (pk * NC_D + k) * 128:(pk * NC_D + k + 1) * 128],
                        hfm[:, k * T:(k + 1) * T],
                        start=(k == 0), stop=(k == NC_D - 1)))
                G("tensor", fns, reads=list(r_hfm) + [r_ww[wname][pk]], writes=[r_ps[bank]])
                slot = c % 2
                G("scalar", [lambda e, bank=bank, slot=slot, bwc=bwc, c=c:
                             e.activation(qkraw[:, slot * T:(slot + 1) * T],
                                          ps[bank][:], AF.Identity,
                                          bias=bwc[:, c:c + 1])],
                  reads=[r_ps[bank], r_rows], writes=[r_qkraw[slot]])
            def stage2(c, wname=wname, rdst=rdst, dsttile=dsttile):
                slot = c % 2
                fns2 = [
                    lambda e, slot=slot: e.matmul(
                        ps[6][0:64, :], projb[0:64, :],
                        qkraw[0:64, slot * T:(slot + 1) * T], start=True, stop=True),
                    lambda e, slot=slot: e.matmul(
                        ps[6][64:128, :], projb[64:128, :],
                        qkraw[64:128, slot * T:(slot + 1) * T], start=True, stop=True),
                ]
                G("tensor", fns2, reads=[r_qkraw[slot], r_rows], writes=[r_ps[6]])
                fns3 = [
                    lambda e: e.activation(tqr[:, 0:T], ps[6][:], AF.Exp),
                    lambda e: e.activation(tqr[:, T:2 * T], ps[6][:], AF.Relu),
                ]
                G("scalar", fns3, reads=[r_ps[6]], writes=[r_tqr[0], r_tqr[1]])
                G("vector", [lambda e, c=c, dsttile=dsttile: e.scalar_tensor_tensor(
                    dsttile[:, c * T:(c + 1) * T], tqr[:, 0:T], 1.0,
                    tqr[:, T:2 * T], ALU.min, ALU.add)],
                  reads=[r_tqr[0], r_tqr[1]], writes=[rdst[c]])
                if wname == "wk":
                    def trk(e, c=c):
                        srcT = kp[:, c * T:(c + 1) * T].rearrange("p (t x) -> p t x", t=NTT)
                        dstT = bass.AP(ktmb, c * 128, [[NTT * D, 128], [D, NTT], [1, 128]])
                        return e.dma_start(out=dstT, in_=srcT, transpose=True)
                    G("scalar", [trk], reads=[r_kp[c]], writes=[r_ktm[c]],
                      sem="tdh", dma=True)
            for c in range(NC_D):
                stage1(c)
                if c > 0:
                    stage2(c - 1)
            stage2(NC_D - 1)

        if DBG and s == 0:
            G("sync", [lambda e: e.dma_start(out=dbg["qp"], in_=qp[:])], reads=[r_qp[7]], sem="od", dma=True)
            G("sync", [lambda e: e.dma_start(out=dbg["kp"], in_=kp[:])], reads=[r_kp[7]], sem="od", dma=True)
            G("sync", [lambda e: e.dma_start(out=dbg["vpad"], in_=vpad[:])], reads=[r_vpad[3]], sem="od", dma=True)
            G("sync", [lambda e: e.dma_start(out=dbg["ktm"], in_=ktmb[:])], reads=list(r_ktm), sem="od", dma=True)
        ktm = ktmb
        for ch in range(NTT):
            seq_ch = (s * NTT + ch) % (4 * NTT)
            par_r = seq_ch % 2
            par_w = (seq_ch + 1) % 2

            def stage1(pr, ch=ch):
                c = pr
                qoff = c * T + ch * 128
                se = 0 if pr % 2 == 0 else 6
                so = 1 if pr % 2 == 0 else 7
                G("tensor", [lambda e, qoff=qoff, se=se: e.matmul(
                    ps[se][:, 0:128],
                    kp[0:64, qoff:qoff + 128], qp[0:64, qoff:qoff + 128],
                    start=True, stop=True)],
                  reads=[r_kp[c], r_qp[c]], writes=[r_ps[se]])
                G("tensor", [lambda e, qoff=qoff, so=so: e.matmul(
                    ps[so][:, 0:128],
                    kp[64:128, qoff:qoff + 128], qp[64:128, qoff:qoff + 128],
                    start=True, stop=True)],
                  reads=[r_kp[c], r_qp[c]], writes=[r_ps[so]])
                sb_off = (pr % 2) * 512
                ssb = tqr[:, sb_off:sb_off + 256]
                G("vector", [
                    lambda e, ssb=ssb, se=se: e.tensor_tensor(
                        ssb[:, 0:128], ps[se][:, 0:128], maskb[:, 0:128], ALU.mult),
                    lambda e, ssb=ssb, so=so: e.tensor_tensor(
                        ssb[:, 128:256], ps[so][:, 0:128], maskb[:, 0:128], ALU.mult)],
                  reads=[r_ps[se], r_ps[so], r_rows], writes=[r_tqr[pr % 2]])

            def stage2(pr, ch=ch, seq_ch=seq_ch, par_r=par_r, par_w=par_w):
                c = pr
                qoff = c * T + ch * 128
                sb_off = (pr % 2) * 512
                ssb = tqr[:, sb_off:sb_off + 256]
                kv_r = pr * 130 + par_r * 65
                kv_w = pr * 130 + par_w * 65
                fns = [lambda e, ssb=ssb, ch=ch, pr=pr, st=(seq_ch == 0): e.matmul(
                    ps[2][:, 0:65], ssb[:, 0:128],
                    vpad[:, ch * H * 65 + (pr * 2) * 65: ch * H * 65 + (pr * 2) * 65 + 65],
                    start=True, stop=st)]
                if seq_ch != 0:
                    fns.append(lambda e, qoff=qoff, kv_r=kv_r: e.matmul(
                        ps[2][:, 0:65], qp[0:64, qoff:qoff + 128],
                        kvb[0:64, kv_r:kv_r + 65], start=False, stop=True))
                rd = [r_tqr[pr % 2], r_vpad[ch], r_qp[c]]
                if seq_ch != 0:
                    rd.append(r_kvs[pr][par_r])
                G("tensor", fns, reads=rd, writes=[r_ps[2]])
                G("tensor", [lambda e, ssb=ssb, ch=ch, pr=pr: e.matmul(
                    ps[3][:, 0:65], ssb[:, 128:256],
                    vpad[:, ch * H * 65 + (pr * 2 + 1) * 65: ch * H * 65 + (pr * 2 + 1) * 65 + 65],
                    start=True, stop=True)],
                  reads=[r_tqr[pr % 2], r_vpad[ch]], writes=[r_ps[3]])
                if seq_ch != 0:
                    G("tensor", [lambda e, qoff=qoff, kv_r=kv_r: e.matmul(
                        ps[4][:, 0:65], qp[64:128, qoff:qoff + 128],
                        kvb[64:128, kv_r:kv_r + 65], start=True, stop=True)],
                      reads=[r_qp[c], r_kvs[pr][par_r]], writes=[r_ps[4]])
                eslot = otmp[:, (pr % 2) * 65:(pr % 2) * 65 + 65]
                oslot = otmp[:, (2 + pr % 2) * 65:(2 + pr % 2) * 65 + 65]
                G("vector", [lambda e, eslot=eslot: e.tensor_copy(eslot, ps[2][:, 0:65])],
                  reads=[r_ps[2]], writes=[r_otmp[pr % 2]], force=True)
                if seq_ch != 0:
                    G("vector", [lambda e, oslot=oslot: e.tensor_copy(oslot, ps[4][:, 0:65])],
                      reads=[r_ps[4]], writes=[r_otmp[2 + pr % 2]], force=True)
                    G("vector", [lambda e, oslot=oslot: e.tensor_tensor(
                        oslot, oslot, ps[3][:, 0:65], ALU.add)],
                      reads=[r_ps[3]], writes=[r_otmp[2 + pr % 2]], force=True)
                else:
                    G("vector", [lambda e, oslot=oslot: e.tensor_copy(oslot, ps[3][:, 0:65])],
                      reads=[r_ps[3]], writes=[r_otmp[2 + pr % 2]], force=True)
                dn = dsc[:, pr * 4:pr * 4 + 2]
                rdn = dsc[:, pr * 4 + 2:pr * 4 + 4]
                G("vector", [
                    lambda e, dn=dn, eslot=eslot: e.tensor_scalar(
                        dn[:, 0:1], eslot[:, 64:65], 1e-6, None, ALU.add),
                    lambda e, dn=dn, oslot=oslot: e.tensor_scalar(
                        dn[:, 1:2], oslot[:, 64:65], 1e-6, None, ALU.add)],
                  reads=[r_otmp[pr % 2], r_otmp[2 + pr % 2]],
                  writes=[r_dsc[pr]], force=True)
                G("vector", [lambda e, rdn=rdn, dn=dn: e.reciprocal(rdn, dn)],
                  writes=[r_dsc[pr]], force=True)
                head_e = pr * 2
                G("scalar", [
                    lambda e, head_e=head_e, rdn=rdn, eslot=eslot: e.activation(
                        atm[:, head_e * 64:(head_e + 1) * 64],
                        eslot[:, 0:64], AF.Identity, scale=rdn[:, 0:1]),
                    lambda e, head_e=head_e, rdn=rdn, oslot=oslot: e.activation(
                        atm[:, (head_e + 1) * 64:(head_e + 2) * 64],
                        oslot[:, 0:64], AF.Identity, scale=rdn[:, 1:2])],
                  reads=[r_otmp[pr % 2], r_otmp[2 + pr % 2], r_dsc[pr]],
                  writes=[r_atm])
                fns = []
                for hh in range(2):
                    koff = ch * D + c * 128 + hh * 64
                    vb = ch * H * 65 + (pr * 2 + hh) * 65
                    fns.append(lambda e, koff=koff, vb=vb, hh=hh: e.matmul(
                        ps[5][hh * 64:(hh + 1) * 64, 0:65],
                        ktm[:, koff:koff + 64],
                        vpad[:, vb:vb + 65], start=True, stop=True))
                G("tensor", fns, reads=[r_ktm[c], r_vpad[ch]], writes=[r_ps[5]])
                dst = kvs[:, kv_w:kv_w + 65]
                dstb = kvb[:, kv_w:kv_w + 65]
                srcp = ps[5][:, 0:65]
                old_ = kvs[:, kv_r:kv_r + 65]
                rd = [r_ps[5]] + ([r_kvs[pr][par_r]] if seq_ch != 0 else [])
                if seq_ch == 0:
                    G("vector", [lambda e, dst=dst, srcp=srcp: e.tensor_copy(dst, srcp)],
                      reads=rd, writes=[r_kvs[pr][par_w]], force=True)
                else:
                    G("vector", [lambda e, dst=dst, srcp=srcp, old_=old_:
                                 e.tensor_tensor(dst, old_, srcp, ALU.add)],
                      reads=rd, writes=[r_kvs[pr][par_w]], force=True)
                G("vector", [lambda e, dst=dst, dstb=dstb: e.tensor_copy(dstb, dst)],
                  reads=[r_kvs[pr][par_w]], writes=[r_kvs[pr][par_w]], force=True)

            for pr in range(NPAIR):
                stage1(pr)
                if pr > 0:
                    stage2(pr - 1)
            stage2(NPAIR - 1)
            def trat(e, ch=ch):
                src = atm[:, :].rearrange("p (c x) -> p c x", c=NC_D)
                dst = bass.AP(afm, ch * 128, [[NC_D * T, 128], [T, NC_D], [1, 128]])
                return e.dma_start(out=dst, in_=src, transpose=True)
            G("scalar", [trat], reads=[r_atm], writes=[r_afm[ch]], sem="tda", dma=True)

        dgrp = [(tt, nh) for tt in range(NTT) for nh in range(2)]
        for k in range(NC_D):
            pk = k % 4
            load_kchunk(wos_h, r_sw["wos"], k, r_ww["wo"][pk], wo_w, pk)
            for gi, (tt, nh) in enumerate(dgrp):
                bank = gi
                G("tensor", [lambda e, k=k, pk=pk, tt=tt, nh=nh, bank=bank: e.matmul(
                    ps[bank][:],
                    afm[:, k * T + tt * 128: k * T + (tt + 1) * 128],
                    wo_w[:, pk * D + nh * 512: pk * D + (nh + 1) * 512],
                    start=(k == 0), stop=(k == NC_D - 1))],
                  reads=[r_afm[tt], r_ww["wo"][pk]], writes=[r_ps[bank]])
        for gi, (tt, nh) in enumerate(dgrp):
            bank = gi
            G("vector", [lambda e, tt=tt, nh=nh, bank=bank, xoff=xoff:
                         e.scalar_tensor_tensor(
                x2b[:, tt * D + nh * 512: tt * D + (nh + 1) * 512],
                ps[bank][:], 1.0,
                x_ab[:, xoff + tt * D + nh * 512: xoff + tt * D + (nh + 1) * 512],
                ALU.mult, ALU.add)],
              reads=[r_ps[bank], r_x[xb]], writes=[r_x2])

        if DBG and s == 0:
            G("sync", [lambda e: e.dma_start(out=dbg["x2"], in_=x2b[:])], reads=[r_x2], sem="od", dma=True)
        ln_stats_slab(lambda tt: x2b[:, tt * D:(tt + 1) * D], [r_x2], 8, 12)
        for tt in range(NTT):
            x2t = x2b[:, tt * D:(tt + 1) * D]
            G("vector", [lambda e, x2t=x2t, tt=tt: e.tensor_scalar(
                htm[:], x2t, sc[:, 8 + tt:9 + tt], sc[:, 12 + tt:13 + tt],
                ALU.subtract, ALU.mult)],
              reads=[r_x2, r_stats], writes=[r_htm])
            def trh2(e, tt=tt):
                src = htm[:, :].rearrange("p (c x) -> p c x", c=NC_D)
                dst = bass.AP(kp, tt * 128, [[NC_D * T, 128], [T, NC_D], [1, 128]])
                return e.dma_start(out=dst, in_=src, transpose=True)
            G("scalar", [trh2], reads=[r_htm], writes=list(r_kp), sem="tdq", dma=True)
            G("gpsimd", [lambda e, tt=tt, xoff=xoff: e.tensor_tensor(
                sx4[:, tt * D:(tt + 1) * D],
                x_ab[:, xoff + tt * D: xoff + (tt + 1) * D],
                x2b[:, tt * D:(tt + 1) * D], ALU.add)],
              reads=[r_x[xb], r_x2], writes=[r_sx4])

        h2fm = kp
        for fc in range(NC_F):
            pk = fc % 3
            load_cblocks(w1s_h, r_sw["w1s"], fc, r_ww["w1"][pk], w1_w, pk, q="gpsimd")
            bank = 4 + (fc % 2)
            fns = []
            for k in range(NC_D):
                fns.append(lambda e, k=k, pk=pk, bank=bank: e.matmul(
                    ps[bank][:],
                    w1_w[:, (pk * NC_D + k) * 128:(pk * NC_D + k + 1) * 128],
                    h2fm[:, k * T:(k + 1) * T],
                    start=(k == 0), stop=(k == NC_D - 1)))
            G("tensor", fns, reads=list(r_kp) + [r_ww["w1"][pk]], writes=[r_ps[bank]])
            G("scalar", [lambda e, fc=fc, bank=bank: e.activation(
                ufm[:, fc * T:(fc + 1) * T], ps[bank][:], AF.Gelu,
                bias=gelb[:, fc:fc + 1])],
              reads=[r_ps[bank], r_rows], writes=[r_ufm[fc]])
        for fc in range(NC_F):
            pk = fc % 4
            load_kchunk(w2s_h, r_sw["w2s"], fc, r_ww["w2"][pk], w2_w, pk)
            for gi, (tt, nh) in enumerate([(t_, n_) for t_ in range(NTT) for n_ in range(2)]):
                bank = gi
                fns = [lambda e, fc=fc, pk=pk, tt=tt, nh=nh, bank=bank: e.matmul(
                    ps[bank][:],
                    ufm[:, fc * T + tt * 128: fc * T + (tt + 1) * 128],
                    w2_w[:, pk * D + nh * 512: pk * D + (nh + 1) * 512],
                    start=(fc == 0), stop=False)]
                if fc == NC_F - 1:
                    fns.append(lambda e, nh=nh, bank=bank: e.matmul(
                        ps[bank][:], ones_c[:],
                        b2_r[:, nh * 512:(nh + 1) * 512],
                        start=False, stop=True))
                G("tensor", fns,
                  reads=[r_ufm[fc], r_ww["w2"][pk], r_rows],
                  writes=[r_ps[bank]])
        for gi, (tt, nh) in enumerate([(t_, n_) for t_ in range(NTT) for n_ in range(2)]):
            bank = gi
            G("vector", [lambda e, tt=tt, nh=nh, bank=bank: e.scalar_tensor_tensor(
                x2b[:, tt * D + nh * 512: tt * D + (nh + 1) * 512],
                ps[bank][:], 1.0,
                sx4[:, tt * D + nh * 512: tt * D + (nh + 1) * 512],
                ALU.mult, ALU.add)],
              reads=[r_ps[bank], r_sx4], writes=[r_x2])

        for k in range(NC_D):
            def f8l(e, k=k):
                return e.dma_start(out=ufm[:, k * 1024:(k + 1) * 1024],
                                   in_=w8s_h[k])
            G("sync", [f8l], reads=[r_sw["w8s"]],
              writes=[r_ufm[2 * k], r_ufm[2 * k + 1]], sem="w8l", dma=True)

        ln_stats_slab(lambda tt: x2b[:, tt * D:(tt + 1) * D], [r_x2], 16, 20)
        for tt in range(NTT):
            zt = x2b[:, tt * D:(tt + 1) * D]
            x4t = sx4[:, tt * D:(tt + 1) * D]
            G("vector", [lambda e, zt=zt, x4t=x4t, tt=tt: e.tensor_scalar(
                x4t, zt, sc[:, 16 + tt:17 + tt], sc[:, 20 + tt:21 + tt],
                ALU.subtract, ALU.mult)],
              reads=[r_x2, r_stats], writes=[r_sx4])

        if s + 1 < NSLAB:
            phase_A(s + 1)

        def quant_steps(tt, A, rA, B, rB, C, rC, c0, obuf, r_ob):
            x4t = sx4[:, tt * D:(tt + 1) * D]
            lf = x2b[:, tt * D:(tt + 1) * D]
            am = sc[:, c0:c0 + 1]; xs = sc[:, c0 + 1:c0 + 2]
            c6 = sc[:, c0 + 2:c0 + 3]; c9 = sc[:, c0 + 3:c0 + 4]
            yield ("vector", [lambda e: e.scalar_tensor_tensor(
                A, x4t, -1.0, x4t, ALU.mult, ALU.max)],
                [r_sx4], [rA], False)
            yield ("vector", [lambda e: e.tensor_scalar(
                B, A, 1e-30, None, ALU.max, ALU.max, accum_out=am)],
                [rA], [rB, r_stats], True)
            yield ("vector", [lambda e: e.tensor_scalar(
                xs, am, 1e-12, 1.0 / 448.0, ALU.max, ALU.mult)],
                [], [r_stats], True)
            yield ("scalar", [
                lambda e: e.activation(c6, xs, AF.Ln, bias=tinyc[:],
                                       scale=2.0 ** -6),
                lambda e: e.activation(c9, xs, AF.Ln, bias=tinyc[:],
                                       scale=2.0 ** -9),
                lambda e: e.activation(lf, B, AF.Ln, bias=tinyc[:])],
                [rB, r_rows, r_x2], [r_stats, r_x2], True)
            yield ("vector", [lambda e: e.tensor_scalar(
                B, lf, c6, INV_LN2, ALU.subtract, ALU.mult)],
                [r_x2, r_stats], [rB], False)
            for w in (8.0, 4.0, 2.0, 1.0):
                yield ("vector", [lambda e, w=w: e.tensor_scalar(
                    A, B, w, w, ALU.is_ge, ALU.mult)], [rB], [rA], False)
                yield ("vector", [lambda e: e.tensor_tensor(
                    B, B, A, ALU.subtract)], [rA], [rB], False)
            yield ("vector", [lambda e: e.tensor_scalar(
                C, lf, c6, INV_LN2, ALU.subtract, ALU.mult)],
                [r_x2, r_stats], [rC], False)
            yield ("vector", [lambda e: e.tensor_tensor(
                C, C, B, ALU.subtract)], [rB], [rC], False)
            yield ("scalar", [
                lambda e: e.activation(A, B, AF.Exp, scale=LN2),
                lambda e: e.activation(C, C, AF.Exp, bias=c9, scale=LN2)],
                [rB, rC, r_stats], [rA, rC], False)
            yield ("vector", [lambda e: e.tensor_scalar(
                B, A, 8.0, 0.5, ALU.mult, ALU.add)], [rA], [rB], False)
            yield ("vector", [lambda e: e.tensor_copy(A, B)],
                [rB], [rA], False)
            for w in (16.0, 8.0, 4.0, 2.0, 1.0):
                yield ("vector", [lambda e, w=w: e.tensor_scalar(
                    lf, A, w, w, ALU.is_ge, ALU.mult)],
                    [rA, r_x2], [r_x2], False)
                yield ("vector", [lambda e: e.tensor_tensor(
                    A, A, lf, ALU.subtract)], [r_x2], [rA], False)
            yield ("vector", [lambda e: e.tensor_tensor(
                B, B, A, ALU.subtract)], [rA], [rB], False)
            yield ("vector", [lambda e: e.tensor_tensor(
                B, B, C, ALU.mult)], [rC], [rB], False)
            yield ("vector", [lambda e: e.tensor_scalar(
                A, x4t, 0.0, 2.0, ALU.is_ge, ALU.mult)],
                [r_sx4], [rA], False)
            yield ("vector", [lambda e: e.tensor_scalar(
                A, A, -1.0, None, ALU.add)], [], [rA], False)
            yield ("vector", [lambda e: e.tensor_tensor(
                obuf[:], B, A, ALU.mult)], [rB, rA], [r_ob], False)

        for ttA in (0, 2):
            ttB = ttA + 1
            gA = quant_steps(ttA, qs[:, 0:D], r_qs[0], qs[:, D:2 * D],
                             r_qs[1], qs[:, 2 * D:3 * D], r_qs[2],
                             68, htm, r_htm)
            gB = quant_steps(ttB, qs[:, 3 * D:4 * D], r_qs[3],
                             fw[:, 0:1024], r_fw[0], fw[:, 1024:2048],
                             r_fw[1], 88, atm, r_atm)
            for stA, stB in zip(gA, gB):
                for (eng, fns, rds, wrs, fc) in (stA, stB):
                    G(eng, fns, reads=rds, writes=wrs, force=fc)
            for tt, obuf, r_ob in ((ttA, htm, r_htm), (ttB, atm, r_atm)):
                def tra2(e, tt=tt, obuf=obuf):
                    src = obuf[:, :].rearrange("p (c x) -> p c x", c=NC_D)
                    dst = bass.AP(afm, tt * 128,
                                  [[NC_D * T, 128], [T, NC_D], [1, 128]])
                    return e.dma_start(out=dst, in_=src, transpose=True)
                G("scalar", [tra2], reads=[r_ob], writes=[r_afm[tt]],
                  sem="tda", dma=True)

        if DBG and s == 0:
            G("sync", [lambda e: e.dma_start(out=dbg["x4"], in_=sx4[:])], reads=[r_sx4], sem="od", dma=True)
            G("sync", [lambda e: e.dma_start(out=dbg["a0"], in_=htm[:])], reads=[r_htm], sem="od", dma=True)
        if s + 1 < NSLAB:
            phase_V()
        for gi, (tt, nh) in enumerate(
                [(t_, n_) for t_ in range(NTT) for n_ in range(2)]):
            bank = gi % 4
            fns = []
            for k in range(NC_D):
                fns.append(lambda e, k=k, tt=tt, nh=nh, bank=bank: e.matmul(
                    ps[bank][:],
                    afm[:, k * T + tt * 128: k * T + (tt + 1) * 128],
                    ufm[:, k * 1024 + nh * 512: k * 1024 + (nh + 1) * 512],
                    start=(k == 0), stop=False))
            fns.append(lambda e, nh=nh, bank=bank: e.matmul(
                ps[bank][:], ones_c[:], bdw_r[:, nh * 512:(nh + 1) * 512],
                start=False, stop=True))
            G("tensor", fns, reads=[r_afm[tt], r_rows] + r_ufm[:2 * NC_D],
              writes=[r_ps[bank]])
            G("vector", [lambda e, tt=tt, nh=nh, bank=bank: e.scalar_tensor_tensor(
                x2b[:, tt * D + nh * 512: tt * D + (nh + 1) * 512],
                ps[bank][:], wscol[:],
                sx4[:, tt * D + nh * 512: tt * D + (nh + 1) * 512],
                ALU.mult, ALU.add)],
              reads=[r_ps[bank], r_sx4, r_rows], writes=[r_x2])

        ln_stats_slab(lambda tt: x2b[:, tt * D:(tt + 1) * D], [r_x2], 24, 28)
        for tt in range(NTT):
            z5t = x2b[:, tt * D:(tt + 1) * D]
            G("vector", [lambda e, z5t=z5t, tt=tt: e.tensor_scalar(
                qs[:, 3 * D:4 * D], z5t, sc[:, 24 + tt:25 + tt],
                sc[:, 28 + tt:29 + tt], ALU.subtract, ALU.mult)],
              reads=[r_x2, r_stats], writes=[r_qs[3]])
            def st_out(e, s=s, tt=tt):
                dst = y_h[(s * NTT + tt) * 128:(s * NTT + tt + 1) * 128, :]
                return e.dma_start(out=dst, in_=qs[:, 3 * D:4 * D])
            G("sync", [st_out], reads=[r_qs[3]], sem="od", dma=True)

    final_od = p.sem_v["od"]
    p._app("sync", lambda e, v=final_od: e.wait_ge(sems["od"], v))
    maxg = getattr(build, "max_groups", None)
    if maxg is not None:
        for e_ in ENGS:
            p.ops[e_] = [(g, f) for (g, f) in p.ops[e_] if g <= maxg]
        print(f"[build] truncated to groups <= {maxg}")

    with es:
        with nc.Block() as block:
            @block.sync
            def _(eng):
                for _g, f in p.ops["sync"]:
                    f(eng)

            @block.vector
            def _(eng):
                for _g, f in p.ops["vector"]:
                    f(eng)

            @block.scalar
            def _(eng):
                for _g, f in p.ops["scalar"]:
                    f(eng)

            @block.tensor
            def _(eng):
                for _g, f in p.ops["tensor"]:
                    f(eng)

            @block.gpsimd
            def _(eng):
                for _g, f in p.ops["gpsimd"]:
                    f(eng)
    return nc


def kernel(L0_attn_x, L0_attn_wq, L0_attn_wk, L0_attn_wv, L0_attn_wo,
           L0_attn_proj, L0_attn_w1, L0_attn_b1, L0_attn_w2, L0_attn_b2,
           L0_attn_g1, L0_attn_beta1, L0_attn_g2, L0_attn_beta2,
           L0_ffn_weight, L0_ffn_weight_scale, L0_ffn_bias):
    x = np.ascontiguousarray(np.asarray(L0_attn_x, dtype=np.float32))
    w8t_u8 = np.ascontiguousarray(np.asarray(L0_ffn_weight).T).view(np.uint8)
    ws_col = np.ascontiguousarray(np.broadcast_to(
        np.asarray(L0_ffn_weight_scale, dtype=np.float32).reshape(1, 1),
        (128, 1)))
    mask2 = np.ascontiguousarray(np.triu(np.ones((128, 128), np.float32)))
    row = lambda a: np.ascontiguousarray(np.asarray(a, np.float32).reshape(1, -1))

    common = {
        "wq": np.ascontiguousarray(np.asarray(L0_attn_wq, np.float32)),
        "wk": np.ascontiguousarray(np.asarray(L0_attn_wk, np.float32)),
        "wv": np.ascontiguousarray(np.asarray(L0_attn_wv, np.float32)),
        "wo": np.ascontiguousarray(np.asarray(L0_attn_wo, np.float32)),
        "proj": np.ascontiguousarray(np.asarray(L0_attn_proj, np.float32)),
        "w1": np.ascontiguousarray(np.asarray(L0_attn_w1, np.float32)),
        "b1": row(L0_attn_b1),
        "w2": np.ascontiguousarray(np.asarray(L0_attn_w2, np.float32)),
        "b2": row(L0_attn_b2),
        "g1": row(L0_attn_g1),
        "be1": row(L0_attn_beta1),
        "g2": row(L0_attn_g2),
        "be2": row(L0_attn_beta2),
        "w8t": w8t_u8,
        "fbias": row(L0_ffn_bias),
        "wsc": ws_col,
        "mask2": mask2,
    }
    in_maps = []
    per_core_b = B // N_CORES
    for c in range(N_CORES):
        mm = dict(common)
        mm["x"] = np.ascontiguousarray(
            x[c * per_core_b:(c + 1) * per_core_b].reshape(TOK, D))
        in_maps.append(mm)

    nc = bass.Bass("TRN2", target_bir_lowering=False, detect_race_conditions=False)
    build(nc)
    res = run_bass_kernel_spmd(nc, in_maps, core_ids=list(range(N_CORES)))
    out = np.empty((B, S, D), np.float32)
    for c in range(N_CORES):
        out[c * per_core_b:(c + 1) * per_core_b] = (
            res.results[c]["y"].reshape(per_core_b, S, D))
    return out
